# version 07
# baseline (speedup 1.0000x reference)
import numpy as np
import ml_dtypes
from contextlib import ExitStack

import concourse.bass as bass
import concourse.mybir as mybir
from concourse.bass import AP
from concourse.bass_utils import run_bass_kernel_spmd

F32 = mybir.dt.float32
BF16 = mybir.dt.bfloat16
I16 = mybir.dt.int16
AF = mybir.ActivationFunctionType
ALU = mybir.AluOpType
AX = mybir.AxisListType

NCORE = 8
B, S, D = 4, 4096, 1024
TPC = 2048
HALO = 512
EXT = TPC + HALO
NPT = D // 128
H, HD = 16, 64
CH = 512
NCH = TPC // CH
ECH = EXT // CH


class Prog:
    def __init__(self, nc, st, names):
        self.h = {n: st.enter_context(nc.semaphore(n)) for n in names}
        self.c = {n: 0 for n in names}

    def sig(self, inst, name, amt=1):
        self.c[name] += amt
        inst.then_inc(self.h[name], amt)
        return self.c[name]

    def dsig(self, inst, name):
        return self.sig(inst, name, 16)

    def wait(self, eng, name, val=None):
        eng.wait_ge(self.h[name], self.c[name] if val is None else val)


def emit_fp8_qdq(v, src, dst, scol, iscol, s0, s1, s2, sb, i0, i1):
    v.tensor_scalar(s0, src, scol, None, ALU.mult)
    v.tensor_copy(sb, s0)
    v.tensor_scalar(i0, sb.bitcast(I16), 7, None, ALU.logical_shift_right)
    v.tensor_scalar(i0, i0, 255, None, ALU.bitwise_and)
    v.tensor_scalar(s1, i0, 121.0, 135.0, ALU.max, ALU.min)
    v.tensor_scalar(i0, s1, 128.0, -384.0, ALU.mult, ALU.add)
    v.tensor_scalar(i1, s1, -128.0, 32896.0, ALU.mult, ALU.add)
    v.tensor_copy(s2, i0.bitcast(BF16))
    v.tensor_copy(s1, i1.bitcast(BF16))
    v.tensor_tensor(s1, s0, s1, ALU.mult)
    v.tensor_scalar(s1, s1, 16.5, None, ALU.add)
    v.tensor_copy(i0, s1)
    v.tensor_tensor(s0, i0, s1, ALU.is_gt)
    v.scalar_tensor_tensor(s1, i0, 16.0, s0, ALU.subtract, ALU.subtract)
    if iscol is None:
        return v.tensor_tensor(dst, s1, s2, ALU.mult)
    v.tensor_tensor(s1, s1, s2, ALU.mult)
    return v.tensor_scalar(dst, s1, iscol, None, ALU.mult)


def amax_to_scalar(P, block, colap, rowscr, dramoff, bncS, name_sfx):
    @block.gpsimd
    def _(g):
        P.wait(g, "tV")
        inst = g.dma_start(out=AP(bncS, dramoff, [[1, 128]]), in_=colap)
        P.dsig(inst, "dG")
        g.wait_ge(P.h["dG"], P.c["dG"])
        inst = g.dma_start(out=rowscr[0:1, 0:128],
                           in_=AP(bncS, dramoff, [[128, 1], [1, 128]]))
        P.dsig(inst, "dG")

    @block.vector
    def _(v):
        P.wait(v, "dG")
        inst = v.tensor_reduce(rowscr[0:1, 128:129], rowscr[0:1, 0:128], AX.X,
                               ALU.max, apply_absolute_value=True)
        P.sig(inst, "tV")


def build_program(sim_nocc=False, dbg=False, dbg_hw=False):
    nc = bass.Bass("TRN2", target_bir_lowering=False, debug=False,
                   detect_race_conditions=False)

    xT = nc.dram_tensor("xT", [D, EXT], F32, kind="ExternalInput")
    w_qkv = nc.dram_tensor("w_qkv", [D, 3 * D], F32, kind="ExternalInput")
    w_out = nc.dram_tensor("w_out", [D, D], F32, kind="ExternalInput")
    W0d = nc.dram_tensor("W0", [D, D], F32, kind="ExternalInput")
    b0d = nc.dram_tensor("b0", [128, 8], F32, kind="ExternalInput")
    m0d = nc.dram_tensor("m0", [128, 4], F32, kind="ExternalInput")
    outT = nc.dram_tensor("outT", [D, TPC], F32, kind="ExternalOutput")

    bncA = nc.dram_tensor("bncA", [8, EXT], F32)
    bncS = nc.dram_tensor("bncS", [1, 512], F32)
    xnf32 = nc.dram_tensor("xnf32", [D, TPC], F32)

    Wq1d = nc.dram_tensor("Wq1", [D, D], F32, kind="ExternalInput")
    Wk1d = nc.dram_tensor("Wk1", [256, D], F32, kind="ExternalInput")
    Wv1d = nc.dram_tensor("Wv1", [256, D], F32, kind="ExternalInput")
    Wo1d = nc.dram_tensor("Wo1", [D, D], F32, kind="ExternalInput")
    EdTd = nc.dram_tensor("EdT", [64, 255], F32, kind="ExternalInput")
    pr1d = nc.dram_tensor("pr1", [128, 2], F32, kind="ExternalInput")
    Wu1d = nc.dram_tensor("Wu1", [3072, D], F32, kind="ExternalInput")
    Wg1d = nc.dram_tensor("Wg1", [3072, D], F32, kind="ExternalInput")
    Wd1d = nc.dram_tensor("Wd1", [D, 3072], F32, kind="ExternalInput")
    agUi = nc.dram_tensor("agUi", [384, D], BF16)
    agUo = nc.dram_tensor("agUo", [3072, D], BF16)
    agGi = nc.dram_tensor("agGi", [384, D], BF16)
    agGo = nc.dram_tensor("agGo", [3072, D], BF16)
    agDi = nc.dram_tensor("agDi", [128, 3072], BF16)
    agDo = nc.dram_tensor("agDo", [D, 3072], BF16)
    upb = nc.dram_tensor("upb", [3072, TPC], BF16)
    gab = nc.dram_tensor("gab", [3072, TPC], BF16)
    tb1 = nc.dram_tensor("tb1", [3072, TPC], BF16)
    ob1 = nc.dram_tensor("ob1", [3072, TPC], BF16)
    wqb1 = nc.dram_tensor("wqb1", [D, D], BF16)
    wkb1 = nc.dram_tensor("wkb1", [256, D], BF16)
    wvb1 = nc.dram_tensor("wvb1", [256, D], BF16)
    wob1 = nc.dram_tensor("wob1", [D, D], BF16)
    agKi = nc.dram_tensor("agKi", [256, TPC], BF16)
    agKo = nc.dram_tensor("agKo", [512, TPC], BF16)
    agVi = nc.dram_tensor("agVi", [TPC, 256], BF16)
    agVo = nc.dram_tensor("agVo", [2 * TPC, 256], BF16)
    PPOFF, PPNR = 640, 1536
    Ppad = nc.dram_tensor("Ppad", [2 * 512, PPNR], BF16)
    bncE = nc.dram_tensor("bncE", [1, 4096], F32)
    wqQb = nc.dram_tensor("wqQb", [D, D], BF16)
    woutb = nc.dram_tensor("woutb", [D, D], BF16)
    w0qb = nc.dram_tensor("w0qb", [D, D], BF16)
    dden = nc.dram_tensor("dden", [H, CH], F32)
    cc_in = nc.dram_tensor("cc_in", [1, 16], F32)
    cc_out = nc.dram_tensor("cc_out", [1, 16], F32)
    dbgd = nc.dram_tensor("dbgd", [128, 65536], F32) if dbg else None
    dbgo = (nc.dram_tensor("dbgo", [128, 2048], F32, kind="ExternalOutput")
            if dbg_hw else None)

    st = ExitStack()
    with st:
        E = st.enter_context
        P = Prog(nc, st, ["dY", "dG", "tT", "tV", "tS", "tG", "cc", "dO"])

        def act_load(s, set_id):
            s.add_instruction(mybir.InstLoadActFuncSet(
                name=nc.get_next_instruction_name(), act_func_set_id=set_id,
                ins=[], outs=[]))

        xres = E(nc.sbuf_tensor("xres", [128, NPT * TPC], F32))
        xn = E(nc.sbuf_tensor("xn", [128, NPT * TPC], BF16))
        wkv = E(nc.sbuf_tensor("wkv", [128, NPT * 2048], BF16))
        rbc = E(nc.sbuf_tensor("rbc", [128, 4 * CH], F32))
        stg = E(nc.sbuf_tensor("stg", [128, 2 * 1024], BF16))
        cols = E(nc.sbuf_tensor("cols", [128, 16], F32))
        red8 = E(nc.sbuf_tensor("red8", [128, 20], F32))
        m0sb = E(nc.sbuf_tensor("m0sb", [128, 4], F32))
        b0sb = E(nc.sbuf_tensor("b0sb", [128, 8], F32))
        ident = E(nc.sbuf_tensor("ident", [128, 128], F32))
        mdiag = E(nc.sbuf_tensor("mdiag", [128, 128], F32))
        onesq = E(nc.sbuf_tensor("onesq", [128, 2], F32))
        estg = E(nc.sbuf_tensor("estg", [128, 4 * 512], BF16))
        block = E(nc.Block())

        def xr(p, a, b_):
            return xres[:, p * TPC + a: p * TPC + b_]

        def xnv(p, a, b_):
            return xn[:, p * TPC + a: p * TPC + b_]

        p0 = ExitStack()
        halo = p0.enter_context(nc.sbuf_tensor("halo", [128, NPT * HALO], F32))
        bcst = p0.enter_context(nc.sbuf_tensor("bcst", [128, 2 * CH], F32))
        sqb = p0.enter_context(nc.sbuf_tensor("sqb", [128, NPT * 512], F32))
        xnh = p0.enter_context(nc.sbuf_tensor("xnh", [128, NPT * HALO], BF16))
        red40 = p0.enter_context(nc.sbuf_tensor("red40", [128, 32], F32))
        rowsc = p0.enter_context(nc.sbuf_tensor("rowsc", [1, 2560], F32))
        pstk = ExitStack()
        pst = pstk.enter_context(nc.psum_tensor("pst", [2, 512], F32))
        pst2 = pstk.enter_context(nc.psum_tensor("pst2", [2, 512], F32))

        @block.sync
        def _(y):
            for p in range(NPT):
                inst = y.dma_start(out=halo[:, p * HALO:(p + 1) * HALO],
                                   in_=xT.ap()[128 * p:128 * (p + 1), 0:HALO])
                P.dsig(inst, "dY")
            for p in range(NPT):
                inst = y.dma_start(out=xr(p, 0, TPC),
                                   in_=xT.ap()[128 * p:128 * (p + 1), HALO:EXT])
                P.dsig(inst, "dY")
            inst = y.dma_start(out=m0sb[:, 0:4], in_=m0d.ap())
            P.dsig(inst, "dY")
            inst = y.dma_start(out=b0sb[:, 0:8], in_=b0d.ap())
            P.dsig(inst, "dY")
        ld_done = P.c["dY"]

        @block.gpsimd
        def _(g):
            g.memset(onesq[:, 0:1], 1.0 / D)
            g.memset(onesq[:, 1:2], 0.0)
            g.memset(ident[:, :], 1.0)
            g.affine_select(ident[:, :], ident[:, :], [[-1, 128]],
                            ALU.is_equal, 0.0, base=0, channel_multiplier=1)
            g.memset(mdiag[:, :], 0.0)
            inst = g.affine_select(mdiag[:, :], mdiag[:, :], [[-1, 128]],
                                   ALU.is_ge, -1e30, base=0, channel_multiplier=1)
            P.sig(inst, "tG")
            for p in range(NPT):
                inst = g.dma_start(out=wkv[:, p * 2048:(p + 1) * 2048],
                                   in_=w_qkv.ap()[128 * p:128 * (p + 1), D:3 * D])
                P.dsig(inst, "dG")
        wkv_done = P.c["dG"]

        cast_cnt, store_cnt = [], []
        for i in range(2 * NPT):
            @block.gpsimd
            def _(g, i=i):
                srcw = (w_qkv.ap()[128 * i:128 * (i + 1), 0:D] if i < NPT else
                        w_out.ap()[128 * (i - NPT):128 * (i - NPT + 1), 0:D])
                if i >= 2:
                    g.wait_ge(P.h["dY"], store_cnt[i - 2])
                sl = i % 2
                inst = g.dma_start(out=stg[:, sl * 1024:(sl + 1) * 1024], in_=srcw)
                cast_cnt.append(P.dsig(inst, "dG"))

            @block.sync
            def _(y, i=i):
                y.wait_ge(P.h["dG"], cast_cnt[i])
                sl = i % 2
                dst = (wqQb.ap()[128 * i:128 * (i + 1), :] if i < NPT else
                       woutb.ap()[128 * (i - NPT):128 * (i - NPT + 1), :])
                inst = y.dma_start(out=dst, in_=stg[:, sl * 1024:(sl + 1) * 1024])
                store_cnt.append(P.dsig(inst, "dY"))
        stage_done = P.c["dY"]

        sq_sig = []
        stat_sig = []
        rcpch = []
        sqrt_sig = []
        mn_free = []
        for ch in range(ECH):
            @block.scalar
            def _(s, ch=ch):
                if ch == 0:
                    s.wait_ge(P.h["dY"], ld_done)
                if ch >= 1:
                    s.wait_ge(P.h["tT"], stat_sig[ch - 1])
                for p in range(NPT):
                    srct = (halo[:, p * HALO:(p + 1) * HALO] if ch == 0 else
                            xr(p, (ch - 1) * CH, ch * CH))
                    inst = s.activation(sqb[:, p * 512:(p + 1) * 512], srct,
                                        AF.Square)
                sq_sig.append(P.sig(inst, "tS"))

            @block.tensor
            def _(t, ch=ch):
                t.wait_ge(P.h["tS"], sq_sig[ch])
                if ch >= 1:
                    t.wait_ge(P.h["tV"], rcpch[ch - 1])
                for p in range(NPT):
                    srct = (halo[:, p * HALO:(p + 1) * HALO] if ch == 0 else
                            xr(p, (ch - 1) * CH, ch * CH))
                    t.matmul(pst[0:2, 0:512], onesq[:, 0:2], srct,
                             start=(p == 0), stop=(p == NPT - 1))
                    inst = t.matmul(pst2[0:2, 0:512], onesq[:, 0:2],
                                    sqb[:, p * 512:(p + 1) * 512],
                                    start=(p == 0), stop=(p == NPT - 1))
                stat_sig.append(P.sig(inst, "tT"))

            @block.vector
            def _(v, ch=ch):
                v.wait_ge(P.h["tT"], stat_sig[ch])
                if ch >= 1:
                    v.wait_ge(P.h["dG"], mn_free[ch - 1])
                v.tensor_copy(rowsc[0:1, 0:512], pst[0:1, 0:512])
                v.tensor_copy(rowsc[0:1, 512:1024], pst2[0:1, 0:512])
                v.tensor_tensor(rowsc[0:1, 1024:1536], rowsc[0:1, 0:512],
                                rowsc[0:1, 0:512], ALU.mult)
                v.tensor_tensor(rowsc[0:1, 512:1024], rowsc[0:1, 512:1024],
                                rowsc[0:1, 1024:1536], ALU.subtract)
                v.tensor_scalar(rowsc[0:1, 512:1024], rowsc[0:1, 512:1024],
                                1e-5, None, ALU.add)
                inst = v.reciprocal(rowsc[0:1, 512:1024], rowsc[0:1, 512:1024])
                rcpch.append(P.sig(inst, "tV"))

            @block.scalar
            def _(s, ch=ch):
                s.wait_ge(P.h["tV"], rcpch[ch])
                inst = s.activation(rowsc[0:1, 1024:1536], rowsc[0:1, 512:1024],
                                    AF.Sqrt)
                sqrt_sig.append(P.sig(inst, "tS"))

            @block.gpsimd
            def _(g, ch=ch):
                g.wait_ge(P.h["tS"], sqrt_sig[ch])
                inst = g.dma_start(out=bncA.ap()[0:1, ch * 512:(ch + 1) * 512],
                                   in_=rowsc[0:1, 0:512])
                P.dsig(inst, "dG")
                inst = g.dma_start(out=bncA.ap()[1:2, ch * 512:(ch + 1) * 512],
                                   in_=rowsc[0:1, 1024:1536])
                mn_free.append(P.dsig(inst, "dG"))
        bncw = P.c["dG"]

        bc_cnt = []
        xn_sig = []
        spill_sig = []

        @block.gpsimd
        def _(g):
            g.wait_ge(P.h["dG"], bncw)
            for ch in range(ECH):
                if ch >= 2:
                    g.wait_ge(P.h["tV"], xn_sig[(ch - 2) * NPT + NPT - 1])
                sl = ch % 2
                inst = g.dma_start(out=bcst[:, sl * 512:sl * 512 + 512],
                                   in_=AP(bncA, ch * 512, [[0, 128], [1, 512]]))
                P.dsig(inst, "dG")
                inst = g.dma_start(out=rbc[:, sl * 512:sl * 512 + 512],
                                   in_=AP(bncA, EXT + ch * 512, [[0, 128], [1, 512]]))
                bc_cnt.append(P.dsig(inst, "dG"))
                if ch >= 2:
                    for p in range(NPT):
                        g.wait_ge(P.h["tV"], xn_sig[(ch - 1) * NPT + p])
                        inst = g.dma_start(
                            out=xnf32.ap()[p * 128:(p + 1) * 128,
                                           (ch - 2) * 512:(ch - 1) * 512],
                            in_=sqb[:, p * 512:(p + 1) * 512])
                        spill_sig.append(P.dsig(inst, "dG"))

                @block.vector
                def _(v, ch=ch, sl=sl):
                    v.wait_ge(P.h["dG"], bc_cnt[ch])
                    for p in range(NPT):
                        if ch >= 2:
                            v.wait_ge(P.h["dG"], spill_sig[(ch - 2) * NPT + p])
                        sq = sqb[:, p * 512:(p + 1) * 512]
                        bc = bcst[:, sl * 512:sl * 512 + 512]
                        rc = rbc[:, sl * 512:sl * 512 + 512]
                        if ch == 0:
                            v.tensor_tensor(sq, halo[:, p * HALO:(p + 1) * HALO],
                                            bc, ALU.subtract)
                            inst = v.tensor_tensor(
                                xnh[:, p * HALO:(p + 1) * HALO], sq, rc, ALU.mult)
                        else:
                            v.tensor_tensor(sq, xr(p, (ch - 1) * CH, ch * CH),
                                            bc, ALU.subtract)
                            v.tensor_tensor(sq, sq, rc, ALU.mult)
                            v.tensor_copy(xnv(p, (ch - 1) * CH, ch * CH), sq)
                            cix = (ch - 1) * NPT + p
                            inst = v.tensor_reduce(red40[:, cix:cix + 1], sq,
                                                   AX.X, ALU.max,
                                                   apply_absolute_value=True)
                        xn_sig.append(P.sig(inst, "tV"))
            for p in range(NPT):
                g.wait_ge(P.h["tV"], xn_sig[(ECH - 1) * NPT + p])
                inst = g.dma_start(
                    out=xnf32.ap()[p * 128:(p + 1) * 128,
                                   (ECH - 2) * 512:(ECH - 1) * 512],
                    in_=sqb[:, p * 512:(p + 1) * 512])
                spill_sig.append(P.dsig(inst, "dG"))
        spill_done = P.c["dG"]

        @block.vector
        def _(v):
            v.memset(red8[:, 16:18], 0.0)
            v.memset(red8[:, 18:20], 0.0)
            inst = v.tensor_reduce(cols[:, 0:1], red40[:, 0:32], AX.X, ALU.max)
            P.sig(inst, "tV")
        xn_done = P.c["tV"]

        if dbg:
            @block.sync
            def _(y):
                y.wait_ge(P.h["tV"], xn_done)
                inst = y.dma_start(out=AP(dbgd, 0, [[65536, 128], [1, 8192]]),
                                   in_=xn[:, :].bitcast(F32))
                P.dsig(inst, "dY")
                inst = y.dma_start(out=AP(dbgd, 8192, [[65536, 128], [1, 2048]]),
                                   in_=xnh[:, :].bitcast(F32))
                P.dsig(inst, "dY")

        amax_to_scalar(P, block, cols[:, 0:1], rowsc[0:1, 1536:1792], 0, bncS, "c1")

        @block.vector
        def _(v):
            inst = v.memset(rowsc[0:1, 1665:1680], 0.0)
            P.sig(inst, "tV")

        @block.gpsimd
        def _(g):
            P.wait(g, "tV")
            inst = g.dma_start(out=cc_in.ap(), in_=rowsc[0:1, 1664:1680])
            P.dsig(inst, "dG")
            g.wait_ge(P.h["dG"], P.c["dG"])
            if sim_nocc:
                inst = g.dma_start(out=cc_out.ap(), in_=cc_in.ap())
                P.dsig(inst, "dG")
                g.wait_ge(P.h["dG"], P.c["dG"])
            else:
                inst = g.collective_compute("AllReduce", ALU.max,
                                            replica_groups=[list(range(NCORE))],
                                            ins=[cc_in.ap()], outs=[cc_out.ap()])
                P.sig(inst, "cc")
                g.wait_ge(P.h["cc"], P.c["cc"])
            inst = g.dma_start(out=rowsc[0:1, 1680:1696], in_=cc_out.ap())
            P.dsig(inst, "dG")

        @block.gpsimd
        def _(g):
            P.wait(g, "dG")
            inst = g.dma_start(out=AP(bncS, 200, [[1, 1]]), in_=rowsc[0:1, 1680:1681])
            P.dsig(inst, "dG")
            g.wait_ge(P.h["dG"], P.c["dG"])
            inst = g.dma_start(out=cols[:, 8:9],
                               in_=AP(bncS, 200, [[0, 128], [1, 1]]))
            P.dsig(inst, "dG")
        amx0_bc = P.c["dG"]

        @block.vector
        def _(v):
            v.wait_ge(P.h["dG"], amx0_bc)
            v.tensor_scalar(cols[:, 8:9], cols[:, 8:9], 1e-12, None, ALU.max)
            v.memset(red8[:, 16:18], 0.0)
            v.memset(red8[:, 18:20], 0.0)
            v.tensor_copy(cols[:, 9:10], cols[:, 8:9])
            v.memset(red8[:, 16:18], 0.0)
            v.memset(red8[:, 18:20], 0.0)
            v.reciprocal(cols[:, 12:14], cols[:, 8:10])
            v.memset(red8[:, 16:18], 0.0)
            v.memset(red8[:, 18:20], 0.0)
            v.tensor_scalar(cols[:, 2:3], cols[:, 12:13], 448.0, None, ALU.mult)
            inst = v.tensor_scalar(cols[:, 3:4], cols[:, 8:9], 1.0 / 448.0, None,
                                   ALU.mult)
            P.sig(inst, "tV")
        s0_v = P.c["tV"]

        @block.gpsimd
        def _(g):
            g.wait_ge(P.h["tV"], s0_v)
            inst = g.dma_start(out=AP(bncS, 210, [[1, 1]]), in_=cols[0:1, 2:3])
            P.dsig(inst, "dG")
        s0_done = P.c["dG"]

        w0ld = []
        w0amax_sig = []

        @block.sync
        def _(y):
            y.wait_ge(P.h["dY"], stage_done)
            y.wait_ge(P.h["tV"], xn_sig[NPT - 1])
            for p in range(NPT):
                if p >= 2:
                    y.wait_ge(P.h["tV"], w0amax_sig[p - 2])
                inst = y.dma_start(out=halo[:, (p % 2) * 1024:(p % 2 + 1) * 1024],
                                   in_=W0d.ap()[128 * p:128 * (p + 1), :])
                w0ld.append(P.dsig(inst, "dY"))

                @block.vector
                def _(v, p=p):
                    v.wait_ge(P.h["dY"], w0ld[p])
                    inst = v.tensor_reduce(
                        red8[:, 8 + p:9 + p],
                        halo[:, (p % 2) * 1024:(p % 2 + 1) * 1024],
                        AX.X, ALU.max, apply_absolute_value=True)
                    w0amax_sig.append(P.sig(inst, "tV"))

        @block.vector
        def _(v):
            v.memset(red8[:, 16:18], 0.0)
            v.memset(red8[:, 18:20], 0.0)
            inst = v.tensor_reduce(cols[:, 4:5], red8[:, 8:16], AX.X, ALU.max)
            w0amax_sig.append(P.sig(inst, "tV"))

        amax_to_scalar(P, block, cols[:, 4:5], rowsc[0:1, 1792:2048], 256, bncS, "w0")

        @block.gpsimd
        def _(g):
            P.wait(g, "dG")
            P.wait(g, "tV")
            inst = g.dma_start(out=AP(bncS, 204, [[1, 1]]), in_=rowsc[0:1, 1920:1921])
            P.dsig(inst, "dG")
            g.wait_ge(P.h["dG"], P.c["dG"])
            inst = g.dma_start(out=cols[:, 10:11],
                               in_=AP(bncS, 204, [[0, 128], [1, 1]]))
            P.dsig(inst, "dG")
        amxw_bc = P.c["dG"]

        @block.vector
        def _(v):
            v.wait_ge(P.h["dG"], amxw_bc)
            v.tensor_scalar(cols[:, 10:11], cols[:, 10:11], 1e-12, None, ALU.max)
            v.memset(red8[:, 16:18], 0.0)
            v.memset(red8[:, 18:20], 0.0)
            v.tensor_copy(cols[:, 11:12], cols[:, 10:11])
            v.memset(red8[:, 16:18], 0.0)
            v.memset(red8[:, 18:20], 0.0)
            v.reciprocal(cols[:, 14:16], cols[:, 10:12])
            v.memset(red8[:, 16:18], 0.0)
            v.memset(red8[:, 18:20], 0.0)
            v.tensor_scalar(cols[:, 6:7], cols[:, 14:15], 448.0, None, ALU.mult)
            v.tensor_scalar(cols[:, 7:8], cols[:, 10:11], 1.0 / 448.0, None,
                            ALU.mult)
            v.memset(red8[:, 16:18], 0.0)
            v.memset(red8[:, 18:20], 0.0)
            inst = v.tensor_tensor(cols[:, 5:6], cols[:, 3:4], cols[:, 7:8],
                                   ALU.mult)
            P.sig(inst, "tV")
        sW_done_v = P.c["tV"]

        @block.gpsimd
        def _(g):
            g.wait_ge(P.h["tV"], sW_done_v)
            inst = g.dma_start(out=AP(bncS, 211, [[1, 1]]), in_=cols[0:1, 6:7])
            P.dsig(inst, "dG")
        sW_done = P.c["dG"]

        w0q_sig = []
        w0st_sig = []

        @block.sync
        def _(y):
            y.wait_ge(P.h["tV"], w0amax_sig[NPT - 1])
            for p in range(NPT):
                if p >= 2:
                    y.wait_ge(P.h["tV"], w0q_sig[p - 2])
                inst = y.dma_start(out=halo[:, (p % 2) * 1024:(p % 2 + 1) * 1024],
                                   in_=W0d.ap()[128 * p:128 * (p + 1), :])
                w0ld.append(P.dsig(inst, "dY"))

                @block.vector
                def _(v, p=p):
                    v.wait_ge(P.h["dY"], w0ld[NPT + p])
                    v.wait_ge(P.h["dG"], sW_done)
                    if p >= 2:
                        v.wait_ge(P.h["dG"], w0st_sig[p - 2])
                    sl = p % 2
                    inst = emit_fp8_qdq(
                        v, halo[:, sl * 1024:(sl + 1) * 1024],
                        estg[:, sl * 1024:(sl + 1) * 1024],
                        cols[:, 6:7], None,
                        sqb[:, 0:1024], sqb[:, 1024:2048], sqb[:, 2048:3072],
                        halo[:, 2048:2560].bitcast(BF16),
                        halo[:, 2560:3072].bitcast(I16),
                        halo[:, 3072:3584].bitcast(I16))
                    w0q_sig.append(P.sig(inst, "tV"))

                @block.gpsimd
                def _(g, p=p):
                    g.wait_ge(P.h["tV"], w0q_sig[p])
                    sl = p % 2
                    inst = g.dma_start(out=w0qb.ap()[128 * p:128 * (p + 1), :],
                                       in_=estg[:, sl * 1024:(sl + 1) * 1024])
                    w0st_sig.append(P.dsig(inst, "dG"))
        w0q_done = P.c["dG"]

        pstk.close()
        p0.close()

        at = ExitStack()
        KT = at.enter_context(nc.sbuf_tensor("KT", [128, 2 * 8 * 512], BF16))
        Vk = at.enter_context(nc.sbuf_tensor("Vk", [128, 2 * 4 * H * 65], BF16))
        QT = at.enter_context(nc.sbuf_tensor("QT", [128, 8 * 512], BF16))
        aU = at.enter_context(nc.sbuf_tensor("aU", [128, 8 * 512], BF16))
        aU2 = at.enter_context(nc.sbuf_tensor("aU2", [64, 8 * 512], BF16))
        dnsc = at.enter_context(nc.sbuf_tensor("dnsc", [128, 2 * 512], F32))
        atp = ExitStack()
        pS = [[atp.enter_context(nc.psum_tensor("pS%d%d" % (x, i), [128, 512], F32))
               for i in range(2)] for x in range(2)]
        pAA = atp.enter_context(nc.psum_tensor("pAA", [65, 512], F32))
        pAB = atp.enter_context(nc.psum_tensor("pAB", [65, 512], F32))
        pQb = [atp.enter_context(nc.psum_tensor("pQb%d" % i, [128, 512], F32))
               for i in range(2)]

        @block.gpsimd
        def _(g):
            inst = g.memset(
                Vk[:, :].rearrange("p (a b) -> p a b", b=65)[:, :, 64:65], 1.0)
            P.sig(inst, "tG")
        vk_ones = P.c["tG"]

        pq_free = [None, None]
        stg_free = [store_cnt[-2], store_cnt[-1]]
        stg_free = [None, None]
        sS_free = [[None, None], [None, None]]
        est_free = [[None, None], [None, None]]
        accf = [None]

        kvev = {}
        qev = {}
        dbg_gate = [0]

        def kvproj(e):
            sl = e % 2

            def xsrc(dt):
                return (xnh[:, dt * HALO:(dt + 1) * HALO] if e == 0 else
                        xnv(dt, (e - 1) * CH, e * CH))

            for ft in range(NPT):
                bk = ft % 2

                @block.tensor
                def _(t, e=e, sl=sl, ft=ft, bk=bk):
                    t.wait_ge(P.h["tV"], xn_done)
                    t.wait_ge(P.h["dG"], wkv_done)
                    t.wait_ge(P.h["tG"], vk_ones)
                    if pq_free[bk] is not None:
                        t.wait_ge(P.h["tV"], pq_free[bk])
                    for dt in range(NPT):
                        inst = t.matmul(pQb[bk][0:128, 0:512],
                                        wkv[:, dt * 2048 + ft * 128:
                                            dt * 2048 + ft * 128 + 128],
                                        xsrc(dt), start=(dt == 0),
                                        stop=(dt == NPT - 1))
                    P.sig(inst, "tT")

                @block.vector
                def _(v, e=e, sl=sl, ft=ft, bk=bk):
                    P.wait(v, "tT")
                    inst = v.tensor_copy(KT[:, (sl * 8 + ft) * 512:
                                            (sl * 8 + ft) * 512 + 512],
                                         pQb[bk][0:128, 0:512])
                    pq_free[bk] = P.sig(inst, "tV")
            for tt in range(4):
                for vh in range(2):
                    bk = (tt * 2 + vh) % 2

                    @block.tensor
                    def _(t, e=e, sl=sl, tt=tt, vh=vh, bk=bk):
                        if pq_free[bk] is not None:
                            t.wait_ge(P.h["tV"], pq_free[bk])
                        for dt in range(NPT):
                            inst = t.matmul(
                                pQb[bk][0:128, 0:512],
                                xsrc(dt)[:, tt * 128:(tt + 1) * 128],
                                wkv[:, dt * 2048 + 1024 + vh * 512:
                                    dt * 2048 + 1024 + vh * 512 + 512],
                                start=(dt == 0), stop=(dt == NPT - 1))
                        P.sig(inst, "tT")

                    @block.vector
                    def _(v, e=e, sl=sl, tt=tt, vh=vh, bk=bk):
                        P.wait(v, "tT")
                        base = ((sl * 4 + tt) * H + vh * 8) * 65
                        dst = Vk[:, base:base + 8 * 65].rearrange(
                            "p (a b) -> p a b", b=65)[:, :, 0:64]
                        inst = v.tensor_copy(
                            dst, pQb[bk][0:128, 0:512].rearrange(
                                "p (a b) -> p a b", b=64))
                        pq_free[bk] = P.sig(inst, "tV")
            kvev[e] = P.c["tV"]

        def qproj(c):
            for ft in range(NPT):
                sl = ft % 2

                @block.sync
                def _(y, c=c, ft=ft, sl=sl):
                    if stg_free[sl] is not None:
                        y.wait_ge(P.h["tT"], stg_free[sl])
                    for dt in range(NPT):
                        inst = y.dma_start(
                            out=stg[:, sl * 1024 + dt * 128:
                                sl * 1024 + (dt + 1) * 128],
                            in_=wqQb.ap()[dt * 128:(dt + 1) * 128,
                                          ft * 128:(ft + 1) * 128])
                        P.dsig(inst, "dY")

                @block.tensor
                def _(t, c=c, ft=ft, sl=sl):
                    P.wait(t, "dY")
                    bk = ft % 2
                    if pq_free[bk] is not None:
                        t.wait_ge(P.h["tV"], pq_free[bk])
                    for dt in range(NPT):
                        inst = t.matmul(pQb[bk][0:128, 0:512],
                                        stg[:, sl * 1024 + dt * 128:
                                            sl * 1024 + dt * 128 + 128],
                                        xnv(dt, c * CH, (c + 1) * CH),
                                        start=(dt == 0), stop=(dt == NPT - 1))
                    stg_free[sl] = P.sig(inst, "tT")

                @block.vector
                def _(v, c=c, ft=ft, sl=sl):
                    P.wait(v, "tT")
                    bk = ft % 2
                    inst = v.tensor_scalar(QT[:, ft * 512:(ft + 1) * 512],
                                           pQb[bk][0:128, 0:512], 0.125, None,
                                           ALU.mult)
                    pq_free[bk] = P.sig(inst, "tV")
            qev[c] = P.c["tV"]

        def attnchunk(c):
            for hp in range(NPT):
                exp_sig = {}
                av_sig = {}
                for kt in range(8):
                    par = kt % 2
                    qoff = 0 if kt < 4 else (kt - 4) * 128
                    sl = (c + (0 if kt < 4 else 1)) % 2
                    ktt = kt % 4
                    kbase = (sl * 8 + hp) * 512 + ktt * 128

                    @block.tensor
                    def _(t, c=c, hp=hp, kt=kt, par=par, qoff=qoff, kbase=kbase):
                        if dbg_gate[0]:
                            t.wait_ge(P.h["dY"], dbg_gate[0])
                        t.wait_ge(P.h["tV"], max(kvev[c], kvev[c + 1], qev[c]))
                        for x in range(2):
                            if sS_free[x][par] is not None:
                                t.wait_ge(P.h["tS"], sS_free[x][par])
                        for x, (r0, r1) in enumerate(((0, 64), (64, 128))):
                            bank = pS[x][par]
                            inst = t.matmul(
                                bank[0:128, qoff:512],
                                KT[r0:r1, kbase:kbase + 128],
                                QT[r0:r1, hp * 512 + qoff:hp * 512 + 512],
                                start=True, stop=(kt < 4))
                            if kt >= 4:
                                inst = t.matmul(bank[0:128, qoff:qoff + 128],
                                                mdiag[:, :], ident[:, :],
                                                is_transpose=True,
                                                start=False, stop=True)
                        P.sig(inst, "tT")

                    @block.scalar
                    def _(s, c=c, hp=hp, kt=kt, par=par, qoff=qoff):
                        P.wait(s, "tT")
                        if c == 0 and kt <= 1:
                            s.wait_ge(P.h["dG"], w0q_done)
                        for x in range(2):
                            if est_free[x][par] is not None:
                                s.wait_ge(P.h["tT"], est_free[x][par])
                        for x in range(2):
                            bias = (m0sb[:, kt:kt + 1]
                                    if (c == 0 and kt < 4) else 0.0)
                            inst = s.activation(
                                estg[:, (x * 2 + par) * 512 + qoff:
                                     (x * 2 + par) * 512 + 512],
                                pS[x][par][0:128, qoff:512], AF.Exp, bias=bias)
                            sS_free[x][par] = P.sig(inst, "tS")
                        exp_sig[kt] = P.c["tS"]

                    @block.tensor
                    def _(t, c=c, hp=hp, kt=kt, par=par, qoff=qoff):
                        t.wait_ge(P.h["tS"], exp_sig[kt])
                        if kt == 0 and accf[0] is not None:
                            t.wait_ge(P.h["tV"], accf[0])
                        sl = (c + (0 if kt < 4 else 1)) % 2
                        ktt = kt % 4
                        for x, acc in enumerate((pAA, pAB)):
                            h = 2 * hp + x
                            vbase = ((sl * 4 + ktt) * H + h) * 65
                            inst = t.matmul(
                                acc[0:65, qoff:512], Vk[:, vbase:vbase + 65],
                                estg[:, (x * 2 + par) * 512 + qoff:
                                     (x * 2 + par) * 512 + 512],
                                start=(kt == 0), stop=(kt == 7))
                        av_sig[kt] = P.sig(inst, "tT")
                        est_free[0][par] = av_sig[kt]
                        est_free[1][par] = av_sig[kt]

                @block.vector
                def _(v, c=c, hp=hp):
                    v.wait_ge(P.h["tT"], av_sig[7])
                    v.tensor_copy(dnsc[64:65, 0:512], pAA[64:65, 0:512])
                    v.reciprocal(dnsc[64:65, 0:512], dnsc[64:65, 0:512])
                    v.tensor_copy(dnsc[64:65, 512:1024], pAB[64:65, 0:512])
                    inst = v.reciprocal(dnsc[64:65, 512:1024],
                                        dnsc[64:65, 512:1024])
                    P.sig(inst, "tV")

                @block.gpsimd
                def _(g, c=c, hp=hp):
                    P.wait(g, "tV")
                    inst = g.dma_start(out=dden.ap()[2 * hp:2 * hp + 1, :],
                                       in_=dnsc[64:65, 0:512])
                    P.dsig(inst, "dG")
                    inst = g.dma_start(out=dden.ap()[2 * hp + 1:2 * hp + 2, :],
                                       in_=dnsc[64:65, 512:1024])
                    P.dsig(inst, "dG")
                    g.wait_ge(P.h["dG"], P.c["dG"])
                    sl2 = hp % 2
                    inst = g.dma_start(
                        out=rbc[0:64, sl2 * 1024:sl2 * 1024 + 512],
                        in_=AP(dden, (2 * hp) * 512, [[0, 64], [1, 512]]))
                    P.dsig(inst, "dG")
                    inst = g.dma_start(
                        out=rbc[0:64, sl2 * 1024 + 512:sl2 * 1024 + 1024],
                        in_=AP(dden, (2 * hp + 1) * 512, [[0, 64], [1, 512]]))
                    P.dsig(inst, "dG")

                @block.vector
                def _(v, c=c, hp=hp):
                    P.wait(v, "dG")
                    sl2 = hp % 2
                    v.tensor_tensor(aU[0:64, hp * 512:hp * 512 + 512],
                                    pAA[0:64, 0:512],
                                    rbc[0:64, sl2 * 1024:sl2 * 1024 + 512],
                                    ALU.mult)
                    inst = v.tensor_tensor(
                        aU2[0:64, hp * 512:hp * 512 + 512],
                        pAB[0:64, 0:512],
                        rbc[0:64, sl2 * 1024 + 512:sl2 * 1024 + 1024], ALU.mult)
                    accf[0] = P.sig(inst, "tV")

            @block.gpsimd
            def _(g, c=c):
                P.wait(g, "tV")
                inst = g.dma_start(out=aU[64:128, :], in_=aU2[0:64, :])
                P.dsig(inst, "dG")
            au_rdy = P.c["dG"]

            for ft in range(NPT):
                sl = ft % 2

                @block.sync
                def _(y, c=c, ft=ft, sl=sl):
                    if stg_free[sl] is not None:
                        y.wait_ge(P.h["tT"], stg_free[sl])
                    for dt in range(NPT):
                        inst = y.dma_start(
                            out=stg[:, sl * 1024 + dt * 128:
                                sl * 1024 + (dt + 1) * 128],
                            in_=woutb.ap()[dt * 128:(dt + 1) * 128,
                                           ft * 128:(ft + 1) * 128])
                        P.dsig(inst, "dY")

                @block.tensor
                def _(t, c=c, ft=ft, sl=sl):
                    P.wait(t, "dY")
                    t.wait_ge(P.h["dG"], au_rdy)
                    bk = ft % 2
                    if pq_free[bk] is not None:
                        t.wait_ge(P.h["tV"], pq_free[bk])
                    for dt in range(NPT):
                        inst = t.matmul(pQb[bk][0:128, 0:512],
                                        stg[:, sl * 1024 + dt * 128:
                                            sl * 1024 + dt * 128 + 128],
                                        aU[:, dt * 512:(dt + 1) * 512],
                                        start=(dt == 0), stop=(dt == NPT - 1))
                    stg_free[sl] = P.sig(inst, "tT")

                @block.vector
                def _(v, c=c, ft=ft, sl=sl):
                    P.wait(v, "tT")
                    bk = ft % 2
                    inst = v.tensor_tensor(xr(ft, c * CH, (c + 1) * CH),
                                           xr(ft, c * CH, (c + 1) * CH),
                                           pQb[bk][0:128, 0:512], ALU.add)
                    pq_free[bk] = P.sig(inst, "tV")

        kvproj(0)
        kvproj(1)
        qproj(0)
        _dbg_marker = True
        if dbg:
            @block.sync
            def _(y):
                y.wait_ge(P.h["tV"], P.c["tV"])
                inst = y.dma_start(out=AP(dbgd, 10240, [[65536, 128], [1, 4096]]),
                                   in_=KT[:, :].bitcast(F32))
                P.dsig(inst, "dY")
                inst = y.dma_start(out=AP(dbgd, 14336, [[65536, 128], [1, 2048]]),
                                   in_=QT[:, :].bitcast(F32))
                P.dsig(inst, "dY")
                inst = y.dma_start(out=AP(dbgd, 16384, [[65536, 128], [1, 4160]]),
                                   in_=Vk[:, :].bitcast(F32))
                dbg_gate[0] = P.dsig(inst, "dY")
        for c in range(NCH):
            attnchunk(c)
            if c + 2 <= ECH - 1:
                kvproj(c + 2)
            if c + 1 <= NCH - 1:
                qproj(c + 1)
            if dbg:
                @block.sync
                def _(y, c=c):
                    y.wait_ge(P.h["tV"], P.c["tV"])
                    inst = y.dma_start(
                        out=AP(dbgd, 20544 + c * 2048, [[65536, 128], [1, 2048]]),
                        in_=aU[:, :].bitcast(F32))
                    dbg_gate[0] = P.dsig(inst, "dY")
        attn_done_v = P.c["tV"]
        attn_done_g = P.c["dG"]
        atp.close()
        at.close()

        ff = ExitStack()
        w0t = ff.enter_context(nc.sbuf_tensor("w0t", [128, 8 * 1024], BF16))
        xqt = ff.enter_context(nc.sbuf_tensor("xqt", [128, 2 * 4096], BF16))
        xfl = ff.enter_context(nc.sbuf_tensor("xfl", [128, 2 * 512], F32))
        fsc = ff.enter_context(nc.sbuf_tensor("fsc", [128, 3 * 512], F32))
        fscb = ff.enter_context(nc.sbuf_tensor("fscb", [128, 512], BF16))
        fsci = ff.enter_context(nc.sbuf_tensor("fsci", [128, 2 * 512], I16))
        pF = [ff.enter_context(nc.psum_tensor("pF%d" % i, [128, 512], F32))
              for i in range(2)]

        if dbg:
            @block.sync
            def _(y):
                y.wait_ge(P.h["tV"], attn_done_v)
                inst = y.dma_start(out=AP(dbgd, 32768, [[65536, 128], [1, 16384]]),
                                   in_=xres[:, :])
                P.dsig(inst, "dY")

        @block.vector
        def _(v):
            v.wait_ge(P.h["tV"], attn_done_v)
            for of in range(NPT):
                inst = v.tensor_scalar(xr(of, 0, TPC), xr(of, 0, TPC),
                                       b0sb[:, of:of + 1], None, ALU.add)
            P.sig(inst, "tV")

        @block.sync
        def _(y):
            y.wait_ge(P.h["dG"], w0q_done)
            y.wait_ge(P.h["dG"], attn_done_g)
            for dt in range(NPT):
                inst = y.dma_start(out=w0t[:, dt * 1024:(dt + 1) * 1024],
                                   in_=w0qb.ap()[0:1024, dt * 128:(dt + 1) * 128],
                                   transpose=True)
                P.dsig(inst, "dY")
        w0t_ld = P.c["dY"]

        xf_ld = []
        xq_sig = []
        mm_done = []
        pf_free = [None, None]
        for tc in range(4):
            buf = tc % 2

            @block.sync
            def _(y, tc=tc, buf=buf):
                if tc == 0:
                    y.wait_ge(P.h["dG"], spill_done)
                for dt in range(NPT):
                    if tc * NPT + dt >= 2:
                        y.wait_ge(P.h["tV"], xq_sig[tc * NPT + dt - 2])
                    inst = y.dma_start(
                        out=xfl[:, (dt % 2) * 512:(dt % 2) * 512 + 512],
                        in_=xnf32.ap()[dt * 128:(dt + 1) * 128,
                                       tc * 512:(tc + 1) * 512])
                    xf_ld.append(P.dsig(inst, "dY"))

                    @block.vector
                    def _(v, tc=tc, dt=dt, buf=buf):
                        v.wait_ge(P.h["dY"], xf_ld[tc * NPT + dt])
                        if tc >= 2 and dt == 0:
                            v.wait_ge(P.h["tT"], mm_done[tc - 2])
                        inst = emit_fp8_qdq(
                            v, xfl[:, (dt % 2) * 512:(dt % 2) * 512 + 512],
                            xqt[:, buf * 4096 + dt * 512:
                                buf * 4096 + dt * 512 + 512],
                            cols[:, 2:3], None,
                            fsc[:, 0:512], fsc[:, 512:1024], fsc[:, 1024:1536],
                            fscb[:, 0:512], fsci[:, 0:512], fsci[:, 512:1024])
                        xq_sig.append(P.sig(inst, "tV"))

            for of in range(NPT):
                bk = (tc * NPT + of) % 2

                @block.tensor
                def _(t, tc=tc, of=of, bk=bk, buf=buf):
                    t.wait_ge(P.h["dY"], w0t_ld)
                    t.wait_ge(P.h["tV"], xq_sig[tc * NPT + NPT - 1])
                    if pf_free[bk] is not None:
                        t.wait_ge(P.h["tV"], pf_free[bk])
                    for dt in range(NPT):
                        inst = t.matmul(
                            pF[bk][0:128, 0:512],
                            w0t[:, dt * 1024 + of * 128:
                                dt * 1024 + of * 128 + 128],
                            xqt[:, buf * 4096 + dt * 512:
                                buf * 4096 + dt * 512 + 512],
                            start=(dt == 0), stop=(dt == NPT - 1))
                    P.sig(inst, "tT")

                @block.vector
                def _(v, tc=tc, of=of, bk=bk):
                    P.wait(v, "tT")
                    inst = v.scalar_tensor_tensor(
                        xr(of, tc * 512, (tc + 1) * 512), pF[bk][0:128, 0:512],
                        cols[:, 5:6], xr(of, tc * 512, (tc + 1) * 512),
                        ALU.mult, ALU.add)
                    pf_free[bk] = P.sig(inst, "tV")
            mm_done.append(P.c["tT"])
        ffn_done = P.c["tV"]
        ff.close()

        l1 = ExitStack()
        red36 = l1.enter_context(nc.sbuf_tensor("red36", [128, 36], F32))
        rowsc_l1 = l1.enter_context(nc.sbuf_tensor("rowsc_l1", [1, 256], F32))
        l1a = ExitStack()
        l1w = stg
        rw1 = rbc
        sq1 = l1a.enter_context(nc.sbuf_tensor("sq1", [128, 4096], F32))
        bc1 = l1a.enter_context(nc.sbuf_tensor("bc1", [128, 4 * 512], F32))
        l1p = ExitStack()
        pL = [l1p.enter_context(nc.psum_tensor("pL%d" % i, [2, 512], F32))
              for i in range(2)]

        wsrc = ([(Wq1d, wqb1, i) for i in range(8)] +
                [(Wo1d, wob1, i) for i in range(8)] +
                [(Wk1d, wkb1, 0), (Wk1d, wkb1, 1),
                 (Wv1d, wvb1, 0), (Wv1d, wvb1, 1)])
        wcast_cnt, wstore_cnt = [], []
        for i, (srcd, dstd, ti) in enumerate(wsrc):
            @block.gpsimd
            def _(g, i=i, srcd=srcd, ti=ti):
                if i == 0:
                    g.wait_ge(P.h["tV"], ffn_done)
                if i >= 2:
                    g.wait_ge(P.h["dY"], wstore_cnt[i - 2])
                sl = i % 2
                inst = g.dma_start(out=l1w[:, sl * 1024:(sl + 1) * 1024],
                                   in_=srcd.ap()[128 * ti:128 * (ti + 1), :])
                wcast_cnt.append(P.dsig(inst, "dG"))

            @block.sync
            def _(y, i=i, dstd=dstd, ti=ti):
                y.wait_ge(P.h["dG"], wcast_cnt[i])
                sl = i % 2
                inst = y.dma_start(out=dstd.ap()[128 * ti:128 * (ti + 1), :],
                                   in_=l1w[:, sl * 1024:(sl + 1) * 1024])
                wstore_cnt.append(P.dsig(inst, "dY"))
        wstage_done = P.c["dY"]

        l1sq_sig = []
        l1stat_sig = []
        l1rcp = []
        l1sqrt = []
        l1mn = []
        for tc in range(4):
            @block.scalar
            def _(s, tc=tc):
                if tc == 0:
                    s.wait_ge(P.h["tV"], ffn_done)
                if tc >= 1:
                    s.wait_ge(P.h["tT"], l1stat_sig[tc - 1])
                for p in range(NPT):
                    inst = s.activation(sq1[:, p * 512:(p + 1) * 512],
                                        xr(p, tc * 512, (tc + 1) * 512), AF.Square)
                l1sq_sig.append(P.sig(inst, "tS"))

            @block.tensor
            def _(t, tc=tc):
                t.wait_ge(P.h["tS"], l1sq_sig[tc])
                if tc >= 1:
                    t.wait_ge(P.h["tV"], l1rcp[tc - 1])
                for p in range(NPT):
                    t.matmul(pL[0][0:2, 0:512], onesq[:, 0:2],
                             xr(p, tc * 512, (tc + 1) * 512),
                             start=(p == 0), stop=(p == NPT - 1))
                    inst = t.matmul(pL[1][0:2, 0:512], onesq[:, 0:2],
                                    sq1[:, p * 512:(p + 1) * 512],
                                    start=(p == 0), stop=(p == NPT - 1))
                l1stat_sig.append(P.sig(inst, "tT"))

            @block.vector
            def _(v, tc=tc):
                v.wait_ge(P.h["tT"], l1stat_sig[tc])
                if tc >= 1:
                    v.wait_ge(P.h["dG"], l1mn[tc - 1])
                v.tensor_copy(rw1[0:1, 0:512], pL[0][0:1, 0:512])
                v.tensor_copy(rw1[0:1, 512:1024], pL[1][0:1, 0:512])
                v.tensor_tensor(rw1[0:1, 1024:1536], rw1[0:1, 0:512],
                                rw1[0:1, 0:512], ALU.mult)
                v.tensor_tensor(rw1[0:1, 512:1024], rw1[0:1, 512:1024],
                                rw1[0:1, 1024:1536], ALU.subtract)
                v.tensor_scalar(rw1[0:1, 512:1024], rw1[0:1, 512:1024],
                                1e-5, None, ALU.add)
                inst = v.reciprocal(rw1[0:1, 512:1024], rw1[0:1, 512:1024])
                l1rcp.append(P.sig(inst, "tV"))

            @block.scalar
            def _(s, tc=tc):
                s.wait_ge(P.h["tV"], l1rcp[tc])
                inst = s.activation(rw1[0:1, 1024:1536], rw1[0:1, 512:1024],
                                    AF.Sqrt)
                l1sqrt.append(P.sig(inst, "tS"))

            @block.gpsimd
            def _(g, tc=tc):
                g.wait_ge(P.h["tS"], l1sqrt[tc])
                inst = g.dma_start(out=bncA.ap()[2:3, tc * 512:(tc + 1) * 512],
                                   in_=rw1[0:1, 0:512])
                P.dsig(inst, "dG")
                inst = g.dma_start(out=bncA.ap()[3:4, tc * 512:(tc + 1) * 512],
                                   in_=rw1[0:1, 1024:1536])
                l1mn.append(P.dsig(inst, "dG"))

        l1bc = []
        l1xn_sig = []

        @block.gpsimd
        def _(g):
            g.wait_ge(P.h["dG"], l1mn[3])
            for tc in range(4):
                sl = tc % 2
                if tc >= 2:
                    g.wait_ge(P.h["tV"], l1xn_sig[(tc - 2) * NPT + NPT - 1])
                inst = g.dma_start(
                    out=bc1[:, sl * 1024:sl * 1024 + 512],
                    in_=AP(bncA, 2 * EXT + tc * 512, [[0, 128], [1, 512]]))
                P.dsig(inst, "dG")
                inst = g.dma_start(
                    out=bc1[:, sl * 1024 + 512:sl * 1024 + 1024],
                    in_=AP(bncA, 3 * EXT + tc * 512, [[0, 128], [1, 512]]))
                l1bc.append(P.dsig(inst, "dG"))

                @block.vector
                def _(v, tc=tc, sl=sl):
                    v.wait_ge(P.h["dG"], l1bc[tc])
                    for p in range(NPT):
                        sq = sq1[:, p * 512:(p + 1) * 512]
                        v.tensor_tensor(sq, xr(p, tc * 512, (tc + 1) * 512),
                                        bc1[:, sl * 1024:sl * 1024 + 512],
                                        ALU.subtract)
                        v.tensor_tensor(xnv(p, tc * 512, (tc + 1) * 512), sq,
                                        bc1[:, sl * 1024 + 512:sl * 1024 + 1024],
                                        ALU.mult)
                        cix = tc * NPT + p
                        inst = v.tensor_reduce(
                            red36[:, cix:cix + 1],
                            xnv(p, tc * 512, (tc + 1) * 512), AX.X, ALU.max,
                            apply_absolute_value=True)
                        l1xn_sig.append(P.sig(inst, "tV"))

        @block.vector
        def _(v):
            v.memset(red36[:, 32:34], 0.0)
            v.memset(red36[:, 34:36], 0.0)
            inst = v.tensor_reduce(cols[:, 0:1], red36[:, 0:32], AX.X, ALU.max)
            P.sig(inst, "tV")
        xn1_done = P.c["tV"]

        amax_to_scalar(P, block, cols[:, 0:1], rowsc_l1[0:1, 0:256], 0, bncS, "c2")

        @block.vector
        def _(v):
            inst = v.memset(rowsc_l1[0:1, 129:144], 0.0)
            P.sig(inst, "tV")

        @block.gpsimd
        def _(g):
            P.wait(g, "tV")
            inst = g.dma_start(out=cc_in.ap(), in_=rowsc_l1[0:1, 128:144])
            P.dsig(inst, "dG")
            g.wait_ge(P.h["dG"], P.c["dG"])
            if sim_nocc:
                inst = g.dma_start(out=cc_out.ap(), in_=cc_in.ap())
                P.dsig(inst, "dG")
                g.wait_ge(P.h["dG"], P.c["dG"])
            else:
                inst = g.collective_compute("AllReduce", ALU.max,
                                            replica_groups=[list(range(NCORE))],
                                            ins=[cc_in.ap()], outs=[cc_out.ap()])
                P.sig(inst, "cc")
                g.wait_ge(P.h["cc"], P.c["cc"])
            inst = g.dma_start(out=AP(bncS, 220, [[1, 1]]), in_=cc_out.ap()[0:1, 0:1])
            P.dsig(inst, "dG")
            g.wait_ge(P.h["dG"], P.c["dG"])
            inst = g.dma_start(out=cols[:, 8:9],
                               in_=AP(bncS, 220, [[0, 128], [1, 1]]))
            P.dsig(inst, "dG")
        c2_bc = P.c["dG"]

        @block.vector
        def _(v):
            v.wait_ge(P.h["dG"], c2_bc)
            v.tensor_scalar(cols[:, 8:9], cols[:, 8:9], 1e-12, None, ALU.max)
            v.memset(red36[:, 32:34], 0.0)
            v.memset(red36[:, 34:36], 0.0)
            v.tensor_copy(cols[:, 9:10], cols[:, 8:9])
            v.memset(red36[:, 32:34], 0.0)
            v.memset(red36[:, 34:36], 0.0)
            v.reciprocal(cols[:, 12:14], cols[:, 8:10])
            v.memset(red36[:, 32:34], 0.0)
            v.memset(red36[:, 34:36], 0.0)
            v.tensor_scalar(cols[:, 2:3], cols[:, 8:9], 1.0 / 32767.0, None,
                            ALU.mult)
            inst = v.tensor_scalar(cols[:, 3:4], cols[:, 12:13], 32767.0, None,
                                   ALU.mult)
            P.sig(inst, "tV")
        c2_done_v = P.c["tV"]

        wq1t = l1a.enter_context(nc.sbuf_tensor("wq1t", [128, 8 * 1024], BF16))
        wkvt = l1a.enter_context(nc.sbuf_tensor("wkvt", [128, 2 * 2048], BF16))
        kown = sq1[:, 0:2048].bitcast(BF16)
        vown = sq1[:, 2048:4096].bitcast(BF16)
        pQ1 = [l1p.enter_context(nc.psum_tensor("pQ1%d" % i, [128, 512], F32))
               for i in range(2)]

        @block.sync
        def _(y):
            y.wait_ge(P.h["dY"], wstage_done)
            for it in range(NPT):
                inst = y.dma_start(out=wq1t[:, it * 1024:(it + 1) * 1024],
                                   in_=wqb1.ap()[0:1024, it * 128:(it + 1) * 128],
                                   transpose=True)
                P.dsig(inst, "dY")
            for it in range(NPT):
                inst = y.dma_start(out=wkvt[:, it * 256:(it + 1) * 256],
                                   in_=wkb1.ap()[0:256, it * 128:(it + 1) * 128],
                                   transpose=True)
                P.dsig(inst, "dY")
                inst = y.dma_start(out=wkvt[:, 2048 + it * 256:2048 + (it + 1) * 256],
                                   in_=wvb1.ap()[0:256, it * 128:(it + 1) * 128],
                                   transpose=True)
                P.dsig(inst, "dY")
        w1t_ld = P.c["dY"]

        qT1 = wkv
        q1_sig = []
        pq_free = [None, None]
        for ot in range(NPT):
            for tc in range(4):
                bk = (ot * 4 + tc) % 2

                @block.tensor
                def _(t, ot=ot, tc=tc, bk=bk):
                    t.wait_ge(P.h["dY"], w1t_ld)
                    t.wait_ge(P.h["tV"], xn1_done)
                    if pq_free[bk] is not None:
                        t.wait_ge(P.h["tV"], pq_free[bk])
                    for dt in range(NPT):
                        inst = t.matmul(
                            pQ1[bk][0:128, 0:512],
                            wq1t[:, dt * 1024 + ot * 128:dt * 1024 + ot * 128 + 128],
                            xnv(dt, tc * 512, (tc + 1) * 512),
                            start=(dt == 0), stop=(dt == NPT - 1))
                    P.sig(inst, "tT")

                @block.vector
                def _(v, ot=ot, tc=tc, bk=bk):
                    P.wait(v, "tT")
                    inst = v.tensor_scalar(
                        qT1[:, ot * 2048 + tc * 512:ot * 2048 + (tc + 1) * 512],
                        pQ1[bk][0:128, 0:512], 0.125, None, ALU.mult)
                    pq_free[bk] = P.sig(inst, "tV")
                    q1_sig.append(pq_free[bk])
        q1_done = P.c["tV"]

        kv_sig = []
        for ot in range(2):
            for tc in range(4):
                bk = (ot * 4 + tc) % 2

                @block.tensor
                def _(t, ot=ot, tc=tc, bk=bk):
                    if pq_free[bk] is not None:
                        t.wait_ge(P.h["tV"], pq_free[bk])
                    for dt in range(NPT):
                        inst = t.matmul(
                            pQ1[bk][0:128, 0:512],
                            wkvt[:, dt * 256 + ot * 128:dt * 256 + ot * 128 + 128],
                            xnv(dt, tc * 512, (tc + 1) * 512),
                            start=(dt == 0), stop=(dt == NPT - 1))
                    P.sig(inst, "tT")

                @block.vector
                def _(v, ot=ot, tc=tc, bk=bk):
                    P.wait(v, "tT")
                    inst = v.tensor_copy(
                        kown[:, ot * 2048 + tc * 512:ot * 2048 + (tc + 1) * 512],
                        pQ1[bk][0:128, 0:512])
                    pq_free[bk] = P.sig(inst, "tV")
                    kv_sig.append(pq_free[bk])

        for tt in range(16):
            bk = tt % 2

            @block.tensor
            def _(t, tt=tt, bk=bk):
                if pq_free[bk] is not None:
                    t.wait_ge(P.h["tV"], pq_free[bk])
                for dt in range(NPT):
                    inst = t.matmul(
                        pQ1[bk][0:128, 0:256],
                        xnv(dt, tt * 128, (tt + 1) * 128),
                        wkvt[:, 2048 + dt * 256:2048 + (dt + 1) * 256],
                        start=(dt == 0), stop=(dt == NPT - 1))
                P.sig(inst, "tT")

            @block.vector
            def _(v, tt=tt, bk=bk):
                P.wait(v, "tT")
                inst = v.tensor_copy(vown[:, tt * 256:(tt + 1) * 256],
                                     pQ1[bk][0:128, 0:256])
                pq_free[bk] = P.sig(inst, "tV")
                kv_sig.append(pq_free[bk])
        kv_done = P.c["tV"]

        @block.gpsimd
        def _(g):
            g.wait_ge(P.h["tV"], kv_done)
            for ot in range(2):
                inst = g.dma_start(out=agKi.ap()[ot * 128:(ot + 1) * 128, :],
                                   in_=kown[:, ot * 2048:(ot + 1) * 2048])
                P.dsig(inst, "dG")
            for tt in range(16):
                inst = g.dma_start(out=agVi.ap()[tt * 128:(tt + 1) * 128, :],
                                   in_=vown[:, tt * 256:(tt + 1) * 256])
                P.dsig(inst, "dG")
            g.wait_ge(P.h["dG"], P.c["dG"])
            grp = [[0, 1], [2, 3], [4, 5], [6, 7]]
            if sim_nocc:
                inst = g.dma_start(out=agKo.ap()[0:256, :], in_=agKi.ap())
                P.dsig(inst, "dG")
                inst = g.dma_start(out=agKo.ap()[256:512, :], in_=agKi.ap())
                P.dsig(inst, "dG")
                inst = g.dma_start(out=agVo.ap()[0:TPC, :], in_=agVi.ap())
                P.dsig(inst, "dG")
                inst = g.dma_start(out=agVo.ap()[TPC:2 * TPC, :], in_=agVi.ap())
                P.dsig(inst, "dG")
                g.wait_ge(P.h["dG"], P.c["dG"])
            else:
                inst = g.collective_compute("AllGather", ALU.bypass,
                                            replica_groups=grp,
                                            ins=[agKi.ap()], outs=[agKo.ap()])
                P.sig(inst, "cc")
                inst = g.collective_compute("AllGather", ALU.bypass,
                                            replica_groups=grp,
                                            ins=[agVi.ap()], outs=[agVo.ap()])
                P.sig(inst, "cc")
                inst = g.collective_compute("AllReduce", ALU.max,
                                            replica_groups=grp,
                                            ins=[cc_in.ap()], outs=[cc_out.ap()])
                P.sig(inst, "cc")
                g.wait_ge(P.h["cc"], P.c["cc"])
            for ii in range(4):
                inst = g.dma_start(out=upb.ap()[0:512, :],
                                   in_=xnf32.ap()[0:512, :])
                P.dsig(inst, "dG")
            g.wait_ge(P.h["dG"], P.c["dG"])
        kvag_done_g = P.c["dG"]
        kvag_cc = P.c["cc"]

        l1p.close()
        l1a.close()
        l1b = ExitStack()
        kTg = [l1b.enter_context(nc.sbuf_tensor("kTg%d" % g, [65, 4096], BF16))
               for g in range(4)]
        Vb = l1b.enter_context(nc.sbuf_tensor("Vb", [128, 32 * 4 * 65], BF16))
        EdTb = l1b.enter_context(nc.sbuf_tensor("EdTb", [128, 256], BF16))
        pr1 = l1b.enter_context(nc.sbuf_tensor("pr1s", [128, 2], F32))
        antA = mdiag[:, 0:64].bitcast(BF16)
        antB = mdiag[:, 64:128].bitcast(BF16)
        qAg = stg
        bc0t = rbc[:, 0:512]
        dn1 = rbc[:, 512:1024]
        ecol = red36
        pps = l1b.enter_context(nc.sbuf_tensor("pps", [128, 2 * 512], BF16))
        stripb = l1b.enter_context(nc.sbuf_tensor("stripb", [128, 4 * 512], BF16))
        on1 = l1b.enter_context(nc.sbuf_tensor("on1", [128, 640], F32))
        filb = l1b.enter_context(nc.sbuf_tensor("filb", [128, 2 * 640], BF16))

        def fillt_ap(s_):
            return filb[:, (s_ % 2) * 640:(s_ % 2) * 640 + 640]
        l2p = ExitStack()
        pS1 = [l2p.enter_context(nc.psum_tensor("pSL%d" % i, [128, 512], F32))
               for i in range(2)]
        pAV = l2p.enter_context(nc.psum_tensor("pAV", [65, 512], F32))
        pB = [l2p.enter_context(nc.psum_tensor("pB%d" % i, [128, 512], F32))
              for i in range(2)]

        @block.gpsimd
        def _(g):
            inst = g.dma_start(out=pr1[:, :], in_=pr1d.ap())
            P.dsig(inst, "dG")
            inst = g.dma_start(out=EdTb[0:64, 0:255], in_=EdTd.ap())
            P.dsig(inst, "dG")
            inst = g.dma_start(out=EdTb[64:128, 0:255], in_=EdTd.ap())
            P.dsig(inst, "dG")
        cst_ld = P.c["dG"]

        @block.vector
        def _(v):
            v.wait_ge(P.h["dG"], cst_ld)
            v.memset(on1[:, :], 1.0)
            v.tensor_scalar(antA, ident[:, :], pr1[:, 0:1], None, ALU.mult)
            inst = v.tensor_scalar(antB, ident[:, :], pr1[:, 1:2], None,
                                   ALU.mult)
            P.sig(inst, "tV")
        ant_done_v = P.c["tV"]

        @block.sync
        def _(y):
            y.wait_ge(P.h["dG"], kvag_done_g)
            y.wait_ge(P.h["cc"], kvag_cc)
            for g_ in range(4):
                for half in range(2):
                    inst = y.dma_start(
                        out=kTg[g_][0:64, half * TPC:(half + 1) * TPC],
                        in_=agKo.ap()[half * 256 + g_ * 64:
                                      half * 256 + g_ * 64 + 64, :])
                    P.dsig(inst, "dY")
            for kt in range(32):
                for g_ in range(4):
                    inst = y.dma_start(
                        out=Vb[:, (kt * 4 + g_) * 65:(kt * 4 + g_) * 65 + 64],
                        in_=agVo.ap()[kt * 128:(kt + 1) * 128,
                                      g_ * 64:(g_ + 1) * 64])
                    P.dsig(inst, "dY")
        kv_ld = P.c["dY"]

        @block.vector
        def _(v):
            v.wait_ge(P.h["dY"], kv_ld)
            for g_ in range(4):
                v.memset(kTg[g_][64:65, :], 1.0)
            inst = v.memset(
                Vb[:, :].rearrange("p (t o) -> p t o", o=65)[:, :, 64:65], 1.0)
            P.sig(inst, "tV")
        kvones = P.c["tV"]

        @block.gpsimd
        def _(g):
            g.memset(bc0t[0:1, 0:512], 0.0)
            inst = g.dma_start(out=AP(bncE, 3584, [[1, 512]]),
                               in_=bc0t[0:1, 0:512])
            P.dsig(inst, "dG")
            g.wait_ge(P.h["dG"], P.c["dG"])
            for bb in range(2):
                for seg in range(2):
                    inst = g.dma_start(
                        out=AP(Ppad, (bb * 512) * PPNR + 894 + seg * 321,
                               [[PPNR, 512], [1, 321]]),
                        in_=AP(bncE, 3584, [[0, 512], [1, 321]]))
                    P.dsig(inst, "dG")
        ppz_done = P.c["dG"]

        def bval(hf, qc, kt):
            return hf * 2048 + qc * 512 - kt * 128 + 127

        strip_rd = []
        ppw_sig = {}
        av_done = {}
        un_done = []
        exp_free = [None] * 4
        sig_exp = [None, None]
        pb_free = None
        pps_free = None
        combo = 0
        for h in range(16):
            g_ = h // 4
            qtile = h // 2
            qrow = (h % 2) * 64
            for qc in range(4):
                buf = combo % 2
                qTh = qT1[qrow:qrow + 64,
                          qtile * 2048 + qc * 512:qtile * 2048 + (qc + 1) * 512]

                bh_sigs = []
                pp_sigs = []
                for s_ in range(4):
                    @block.tensor
                    def _(t, qTh=qTh, s_=s_, pbf=pb_free, qrow=qrow):
                        if s_ == 0:
                            t.wait_ge(P.h["tV"], q1_done)
                            t.wait_ge(P.h["tV"], kvones)
                            t.wait_ge(P.h["tV"], ant_done_v)
                            if pbf is not None:
                                t.wait_ge(P.h["tV"], pbf)
                        elif s_ >= 2:
                            t.wait_ge(P.h["tV"], bh_sigs[s_ - 2])
                        inst = t.matmul(pB[s_ % 2][0:128, 0:255],
                                        qTh[:, s_ * 128:(s_ + 1) * 128],
                                        EdTb[qrow:qrow + 64, 0:255],
                                        start=True, stop=True)
                        P.sig(inst, "tT")
                    bh_mm = P.c["tT"]

                    @block.vector
                    def _(v, s_=s_, bh_mm=bh_mm, ppf=pps_free):
                        v.wait_ge(P.h["tT"], bh_mm)
                        if s_ == 0 and ppf is not None:
                            v.wait_ge(P.h["dG"], ppf)
                        v.tensor_copy(ecol[:, 2 * s_:2 * s_ + 1],
                                      pB[s_ % 2][0:128, 254:255])
                        v.tensor_copy(ecol[:, 2 * s_ + 1:2 * s_ + 2],
                                      pB[s_ % 2][0:128, 0:1])
                        v.memset(ecol[:, 32:34], 0.0)
                        v.memset(ecol[:, 34:36], 0.0)
                        v.tensor_scalar(pps[:, s_ * 255:(s_ + 1) * 255],
                                        pB[s_ % 2][0:128, 0:255],
                                        ecol[:, 2 * s_:2 * s_ + 1], None,
                                        ALU.subtract)
                        v.tensor_tensor(ecol[:, 16 + s_:17 + s_],
                                        ecol[:, 2 * s_ + 1:2 * s_ + 2],
                                        ecol[:, 2 * s_:2 * s_ + 1],
                                        ALU.subtract)
                        v.memset(ecol[:, 32:34], 0.0)
                        v.memset(ecol[:, 34:36], 0.0)
                        if s_ >= 2:
                            v.wait_ge(P.h["dG"], pp_sigs[s_ - 2])
                        inst = v.tensor_scalar(fillt_ap(s_), on1[:, :],
                                               ecol[:, 16 + s_:17 + s_], None,
                                               ALU.mult)
                        bh_sigs.append(P.sig(inst, "tV"))

                    @block.gpsimd
                    def _(g, s_=s_, buf=buf):
                        g.wait_ge(P.h["tV"], bh_sigs[s_])
                        if s_ == 0:
                            g.wait_ge(P.h["dG"], ppz_done)
                        base = (buf * 512 + s_ * 128) * PPNR
                        inst = g.dma_start(
                            out=AP(Ppad, base + PPOFF, [[PPNR, 128], [1, 255]]),
                            in_=pps[:, s_ * 255:(s_ + 1) * 255])
                        P.dsig(inst, "dG")
                        inst = g.dma_start(
                            out=AP(Ppad, base, [[PPNR, 128], [1, 640]]),
                            in_=fillt_ap(s_))
                        P.dsig(inst, "dG")
                        inst = g.dma_start(out=AP(bncE, s_ * 128, [[1, 128]]),
                                           in_=ecol[:, 2 * s_:2 * s_ + 1])
                        P.dsig(inst, "dG")
                        inst = g.dma_start(
                            out=AP(bncE, 512 + s_ * 128, [[1, 128]]),
                            in_=ecol[:, 2 * s_ + 1:2 * s_ + 2])
                        pp_sigs.append(P.dsig(inst, "dG"))
                ppw = P.c["dG"]
                ppw_sig[combo] = ppw
                pps_free = ppw
                pb_free = bh_sigs[-1]

                @block.gpsimd
                def _(g, ppw=ppw):
                    g.wait_ge(P.h["dG"], ppw)
                    inst = g.dma_start(out=qAg[64:65, 0:512],
                                       in_=AP(bncE, 0, [[1, 512]]))
                    P.dsig(inst, "dG")
                    inst = g.dma_start(out=qAg[64:65, 512:1024],
                                       in_=AP(bncE, 512, [[1, 512]]))
                    P.dsig(inst, "dG")
                bh_bc = P.c["dG"]

                @block.vector
                def _(v, bh_bc=bh_bc, qTh=qTh):
                    v.wait_ge(P.h["dG"], bh_bc)
                    for cl in range(4):
                        v.tensor_copy(qAg[0:64, cl * 512:(cl + 1) * 512], qTh)
                    v.tensor_scalar(qAg[64:65, 1024:1536], qAg[64:65, 0:512],
                                    pr1[64:65, 0:1], None, ALU.mult)
                    v.tensor_scalar(qAg[64:65, 1536:2048], qAg[64:65, 512:1024],
                                    pr1[64:65, 0:1], None, ALU.mult)
                    v.scalar_tensor_tensor(qAg[64:65, 1024:1536],
                                           qAg[64:65, 512:1024],
                                           pr1[64:65, 1:2],
                                           qAg[64:65, 1024:1536],
                                           ALU.mult, ALU.add)
                    inst = v.scalar_tensor_tensor(qAg[64:65, 1536:2048],
                                                  qAg[64:65, 0:512],
                                                  pr1[64:65, 1:2],
                                                  qAg[64:65, 1536:2048],
                                                  ALU.mult, ALU.add)
                    P.sig(inst, "tV")
                qa_built = P.c["tV"]

                cstrips = {}
                cnt = 0
                for kt in range(32):
                    lst = []
                    for hf in range(2):
                        b = bval(hf, qc, kt)
                        if -511 <= b <= 381:
                            lst.append((cnt, hf, b))
                            cnt += 1
                    cstrips[kt] = lst

                for kt in range(32):
                    bank = kt % 2
                    b0v = bval(0, qc, kt)
                    b1v = bval(1, qc, kt)
                    c0 = 1 if b0v > 381 else 0
                    c1 = 1 if b1v > 381 else 0
                    cl = (0 if (c0 == 0 and c1 == 0) else
                          1 if (c0 == 1 and c1 == 1) else
                          2 if (c0 == 0) else 3)
                    my = cstrips[kt]

                    if my:
                        @block.sync
                        def _(y, my=my, buf=buf, ppw=ppw):
                            y.wait_ge(P.h["dG"], ppw)
                            for (si, hf, b) in my:
                                slot = len(strip_rd) % 4
                                if len(strip_rd) >= 4:
                                    y.wait_ge(P.h["tT"], strip_rd[-4][1])
                                for qx in range(4):
                                    c00 = PPOFF + 254 - b - qx * 128
                                    off = ((buf * 512 + qx * 128) * PPNR +
                                           c00 - 127)
                                    inst = y.dma_start(
                                        out=stripb[:, slot * 512 + qx * 128:
                                                   slot * 512 + qx * 128 + 128],
                                        in_=AP(Ppad, off,
                                               [[PPNR - 1, 128], [1, 128]]))
                                    ld = P.dsig(inst, "dY")
                                strip_rd.append([ld, None, slot])

                    @block.tensor
                    def _(t, kt=kt, bank=bank, cl=cl, g_=g_, my=my,
                          qa_built=qa_built, sbase=len(strip_rd) - len(my)):
                        t.wait_ge(P.h["tV"], qa_built)
                        if sig_exp[bank] is not None:
                            t.wait_ge(P.h["tS"], sig_exp[bank])
                        nstr = len(my)
                        inst = t.matmul(pS1[bank][0:128, 0:512],
                                        kTg[g_][0:65, kt * 128:(kt + 1) * 128],
                                        qAg[0:65, cl * 512:(cl + 1) * 512],
                                        start=True, stop=(nstr == 0))
                        for ii, (si, hf, b) in enumerate(my):
                            idp = antA if hf == 0 else antB
                            sr = strip_rd[sbase + ii]
                            t.wait_ge(P.h["dY"], sr[0])
                            for qx in range(4):
                                inst = t.matmul(
                                    pS1[bank][0:128, qx * 128:(qx + 1) * 128],
                                    stripb[:, sr[2] * 512 + qx * 128:
                                           sr[2] * 512 + qx * 128 + 128],
                                    idp,
                                    start=False,
                                    stop=(ii == nstr - 1 and qx == 3))
                            sr[1] = P.sig(inst, "tT")
                        if nstr == 0:
                            P.sig(inst, "tT")

                    @block.scalar
                    def _(s, kt=kt, bank=bank):
                        P.wait(s, "tT")
                        if exp_free[kt % 4] is not None:
                            s.wait_ge(P.h["tT"], exp_free[kt % 4])
                        inst = s.activation(
                            estg[:, (kt % 4) * 512:(kt % 4 + 1) * 512],
                            pS1[bank][0:128, 0:512], AF.Exp)
                        sig_exp[bank] = P.sig(inst, "tS")

                    @block.tensor
                    def _(t, kt=kt, g_=g_):
                        P.wait(t, "tS")
                        inst = t.matmul(pAV[0:65, 0:512],
                                        Vb[:, (kt * 4 + g_) * 65:
                                           (kt * 4 + g_) * 65 + 65],
                                        estg[:, (kt % 4) * 512:(kt % 4 + 1) * 512],
                                        start=(kt == 0), stop=(kt == 31))
                        exp_free[kt % 4] = P.sig(inst, "tT")
                av_done[combo] = P.c["tT"]

                @block.vector
                def _(v, combo=combo):
                    v.wait_ge(P.h["tT"], av_done[combo])
                    v.tensor_copy(dn1[64:65, :], pAV[64:65, 0:512])
                    inst = v.reciprocal(dn1[64:65, :], dn1[64:65, :])
                    P.sig(inst, "tV")

                @block.gpsimd
                def _(g):
                    P.wait(g, "tV")
                    inst = g.dma_start(out=AP(bncE, 1024, [[1, 512]]),
                                       in_=dn1[64:65, :])
                    P.dsig(inst, "dG")
                    g.wait_ge(P.h["dG"], P.c["dG"])
                    inst = g.dma_start(out=dn1[0:64, :],
                                       in_=AP(bncE, 1024, [[0, 64], [1, 512]]))
                    P.dsig(inst, "dG")
                dnb = P.c["dG"]

                @block.vector
                def _(v, dnb=dnb, qrow=qrow, qtile=qtile, qc=qc):
                    v.wait_ge(P.h["dG"], dnb)
                    inst = v.tensor_tensor(
                        qT1[qrow:qrow + 64,
                            qtile * 2048 + qc * 512:qtile * 2048 + (qc + 1) * 512],
                        pAV[0:64, 0:512], dn1[0:64, :], ALU.mult)
                    un_done.append(P.sig(inst, "tV"))
                combo += 1
        attn1_done = P.c["tV"]

        l1b.close()
        wo1t = l1.enter_context(nc.sbuf_tensor("wo1t", [128, 8 * 1024], BF16))

        @block.sync
        def _(y):
            y.wait_ge(P.h["tV"], q1_done)
            y.wait_ge(P.h["dY"], wstage_done)
            for it in range(NPT):
                inst = y.dma_start(out=wo1t[:, it * 1024:(it + 1) * 1024],
                                   in_=wob1.ap()[0:1024, it * 128:(it + 1) * 128],
                                   transpose=True)
                P.dsig(inst, "dY")
        wo1_ld = P.c["dY"]

        op_free = [None, None]
        for ot in range(NPT):
            for tc in range(4):
                bk = (ot * 4 + tc) % 2

                @block.tensor
                def _(t, ot=ot, tc=tc, bk=bk):
                    t.wait_ge(P.h["dY"], wo1_ld)
                    t.wait_ge(P.h["tV"], attn1_done)
                    if op_free[bk] is not None:
                        t.wait_ge(P.h["tV"], op_free[bk])
                    for dt in range(NPT):
                        inst = t.matmul(
                            pS1[bk][0:128, 0:512],
                            wo1t[:, dt * 1024 + ot * 128:dt * 1024 + ot * 128 + 128],
                            qT1[:, dt * 2048 + tc * 512:dt * 2048 + (tc + 1) * 512],
                            start=(dt == 0), stop=(dt == NPT - 1))
                    P.sig(inst, "tT")

                @block.vector
                def _(v, ot=ot, tc=tc, bk=bk):
                    P.wait(v, "tT")
                    inst = v.tensor_tensor(
                        xr(ot, tc * 512, (tc + 1) * 512),
                        xr(ot, tc * 512, (tc + 1) * 512),
                        pS1[bk][0:128, 0:512], ALU.add)
                    op_free[bk] = P.sig(inst, "tV")
        oproj_done = P.c["tV"]

        l2p.close()
        l1.close()
        f1 = ExitStack()
        fcols = f1.enter_context(nc.sbuf_tensor("fcols", [128, 24], F32))
        fred = f1.enter_context(nc.sbuf_tensor("fred", [128, 100], F32))
        f1a = ExitStack()
        w4f = f1a.enter_context(nc.sbuf_tensor("w4f", [128, 3072], F32))
        w4s = f1a.enter_context(nc.sbuf_tensor("w4s", [128, 3072], F32))
        w4i = f1a.enter_context(nc.sbuf_tensor("w4i", [128, 3072], I16))
        w4b = f1a.enter_context(nc.sbuf_tensor("w4b", [128, 3072], BF16))
        fscl = f1a.enter_context(nc.sbuf_tensor("fscl", [128, 224], F32))
        fo = wkv
        f1p = ExitStack()
        pF1 = [f1p.enter_context(nc.psum_tensor("pF1%d" % i, [128, 512], F32))
               for i in range(2)]
        pD = [f1p.enter_context(nc.psum_tensor("pD%d" % i, [128, 512], F32))
              for i in range(6)]

        dq_sig = []
        dq_st = []
        wlist = [(Wu1d, agUo, 24, 1024), (Wg1d, agGo, 24, 1024),
                 (Wd1d, agDo, 8, 3072)]
        dql_sig = []
        for (srcd, dsti, ntile, W) in wlist:
            for ti in range(ntile):
                @block.sync
                def _(y, srcd=srcd, ti=ti, W=W):
                    if not dq_sig:
                        y.wait_ge(P.h["tV"], oproj_done)
                    else:
                        y.wait_ge(P.h["tV"], dq_sig[-1])
                    inst = y.dma_start(out=w4f[:, 0:W],
                                       in_=srcd.ap()[ti * 128:(ti + 1) * 128, :])
                    dql_sig.append(P.dsig(inst, "dY"))

                @block.vector
                def _(v, W=W):
                    v.wait_ge(P.h["dY"], dql_sig[-1])
                    nblk = W // 32
                    v.tensor_reduce(fscl[:, 0:nblk],
                                    w4f[:, 0:W].rearrange("p (a b) -> p a b",
                                                          b=32),
                                    AX.X, ALU.max, apply_absolute_value=True)
                    v.memset(fred[:, 96:98], 0.0)
                    v.memset(fred[:, 98:100], 0.0)
                    v.tensor_scalar(fscl[:, 0:nblk], fscl[:, 0:nblk],
                                    1.0 / 7.0, 1e-12, ALU.mult, ALU.max)
                    v.memset(fred[:, 96:98], 0.0)
                    v.memset(fred[:, 98:100], 0.0)
                    inst = v.reciprocal(fscl[:, 112:112 + nblk], fscl[:, 0:nblk])
                    P.sig(inst, "tV")

                @block.vector
                def _(v, W=W, dsti=dsti, ti=ti):
                    nblk = W // 32
                    v.memset(fred[:, 96:98], 0.0)
                    v.memset(fred[:, 98:100], 0.0)
                    src3 = fscl[:, 112:112 + nblk].rearrange(
                        "p (a b) -> p a b", b=1)
                    d3 = w4s[:, 0:W].rearrange("p (a b) -> p a b", b=32)
                    for i in range(32):
                        v.tensor_copy(d3[:, :, i:i + 1], src3)
                    v.memset(fred[:, 96:98], 0.0)
                    v.memset(fred[:, 98:100], 0.0)
                    v.tensor_tensor(w4f[:, 0:W], w4f[:, 0:W], w4s[:, 0:W],
                                    ALU.mult)
                    v.tensor_copy(w4i[:, 0:W], w4f[:, 0:W])
                    v.tensor_scalar(w4i[:, 0:W], w4i[:, 0:W], -8, None, ALU.max)
                    v.tensor_scalar(w4i[:, 0:W], w4i[:, 0:W], 7, None, ALU.min)
                    src3b = fscl[:, 0:nblk].rearrange("p (a b) -> p a b", b=1)
                    d3b = w4s[:, 0:W].rearrange("p (a b) -> p a b", b=32)
                    for i in range(32):
                        v.tensor_copy(d3b[:, :, i:i + 1], src3b)
                    v.tensor_copy(w4f[:, 0:W], w4i[:, 0:W])
                    inst = v.tensor_tensor(w4f[:, 0:W], w4f[:, 0:W],
                                           w4s[:, 0:W], ALU.mult)
                    dq_sig.append(P.sig(inst, "tV"))

                @block.vector
                def _(v, W=W, prev_st=(dq_st[-1] if dq_st else None)):
                    if prev_st is not None:
                        v.wait_ge(P.h["dG"], prev_st)
                    inst = v.tensor_copy(w4b[:, 0:W], w4f[:, 0:W])
                    dq_sig.append(P.sig(inst, "tV"))

                @block.gpsimd
                def _(g, dsti=dsti, ti=ti, W=W):
                    g.wait_ge(P.h["tV"], dq_sig[-1])
                    inst = g.dma_start(out=dsti.ap()[ti * 128:(ti + 1) * 128, :],
                                       in_=w4b[:, 0:W])
                    dq_st.append(P.dsig(inst, "dG"))

        wag_done = dq_st[-1]
        wag_cc = P.c["cc"]
        f1a.close()
        fwt = f1.enter_context(nc.sbuf_tensor("fwt", [128, 8 * 1536], BF16))
        fa = f1.enter_context(nc.sbuf_tensor("fa", [128, 4 * 512], F32))
        fb = f1.enter_context(nc.sbuf_tensor("fb", [128, 4 * 512], F32))
        fi = f1.enter_context(nc.sbuf_tensor("fi", [128, 4 * 512], I16))

        xq1_sig = []

        @block.vector
        def _(v):
            v.wait_ge(P.h["tV"], oproj_done)
            for dt in range(NPT):
                for ch in range(4):
                    sl = xnv(dt, ch * 512, (ch + 1) * 512)
                    v.tensor_scalar(fa[:, 0:512], sl, cols[:, 3:4], None,
                                    ALU.mult)
                    v.tensor_copy(fi[:, 0:512], fa[:, 0:512])
                    inst = v.tensor_scalar(sl, fi[:, 0:512], cols[:, 2:3],
                                           None, ALU.mult)
                    xq1_sig.append(P.sig(inst, "tV"))
        xq1_done = P.c["tV"]

        def cc_amax(cols_in, cols_out, tag):
            @block.gpsimd
            def _(g):
                P.wait(g, "tV")
                for i, ci in enumerate(cols_in):
                    inst = g.dma_start(out=AP(bncS, 230 + i, [[1, 1]]),
                                       in_=fcols[0:1, ci:ci + 1])
                    P.dsig(inst, "dG")
                g.wait_ge(P.h["dG"], P.c["dG"])
                inst = g.dma_start(out=cc_in.ap(),
                                   in_=AP(bncS, 224, [[1, 16]]))
                P.dsig(inst, "dG")
                g.wait_ge(P.h["dG"], P.c["dG"])
                if sim_nocc:
                    inst = g.dma_start(out=cc_out.ap(), in_=cc_in.ap())
                    P.dsig(inst, "dG")
                    g.wait_ge(P.h["dG"], P.c["dG"])
                else:
                    inst = g.collective_compute(
                        "AllReduce", ALU.max,
                        replica_groups=[list(range(NCORE))],
                        ins=[cc_in.ap()], outs=[cc_out.ap()])
                    P.sig(inst, "cc")
                    g.wait_ge(P.h["cc"], P.c["cc"])
                for i, co in enumerate(cols_out):
                    inst = g.dma_start(out=AP(bncS, 240 + i, [[1, 1]]),
                                       in_=cc_out.ap()[0:1, 6 + i:7 + i])
                    P.dsig(inst, "dG")
                g.wait_ge(P.h["dG"], P.c["dG"])
                for i, co in enumerate(cols_out):
                    inst = g.dma_start(out=fcols[:, co:co + 1],
                                       in_=AP(bncS, 240 + i, [[0, 128], [1, 1]]))
                    P.dsig(inst, "dG")
            return P.c["dG"]

        def scale_cols(amax_col, s_col, inv_col, qmax, gate_cnt):
            @block.vector
            def _(v):
                v.wait_ge(P.h["dG"], gate_cnt)
                v.tensor_scalar(fcols[:, s_col:s_col + 1],
                                fcols[:, amax_col:amax_col + 1],
                                1.0 / qmax, 1e-12, ALU.mult, ALU.max)
                v.memset(fred[:, 96:98], 0.0)
                v.memset(fred[:, 98:100], 0.0)
                v.tensor_copy(fcols[:, s_col + 1:s_col + 2],
                              fcols[:, s_col:s_col + 1])
                v.memset(fred[:, 96:98], 0.0)
                v.memset(fred[:, 98:100], 0.0)
                inst = v.reciprocal(fcols[:, inv_col:inv_col + 2],
                                    fcols[:, s_col:s_col + 2])
                P.sig(inst, "tV")
            return P.c["tV"]

        frow = f1.enter_context(nc.sbuf_tensor("frow", [1, 256], F32))
        fu = f1.enter_context(nc.sbuf_tensor("fu", [128, 2 * 512],
                                             mybir.dt.uint16))

        def l1_amax_cc(col_in, col_out, bcoff):
            amax_to_scalar(P, block, fcols[:, col_in:col_in + 1],
                           frow[0:1, 0:256], 300, bncS, "f%d" % col_in)

            @block.vector
            def _(v):
                inst = v.memset(frow[0:1, 129:144], 0.0)
                P.sig(inst, "tV")

            @block.gpsimd
            def _(g):
                P.wait(g, "tV")
                inst = g.dma_start(out=cc_in.ap(), in_=frow[0:1, 128:144])
                P.dsig(inst, "dG")
                g.wait_ge(P.h["dG"], P.c["dG"])
                if sim_nocc:
                    inst = g.dma_start(out=cc_out.ap(), in_=cc_in.ap())
                    P.dsig(inst, "dG")
                    g.wait_ge(P.h["dG"], P.c["dG"])
                else:
                    inst = g.collective_compute(
                        "AllReduce", ALU.max,
                        replica_groups=[list(range(NCORE))],
                        ins=[cc_in.ap()], outs=[cc_out.ap()])
                    P.sig(inst, "cc")
                    g.wait_ge(P.h["cc"], P.c["cc"])
                inst = g.dma_start(out=AP(bncS, bcoff, [[1, 1]]),
                                   in_=cc_out.ap()[0:1, 0:1])
                P.dsig(inst, "dG")
                g.wait_ge(P.h["dG"], P.c["dG"])
                inst = g.dma_start(out=fcols[:, col_out:col_out + 1],
                                   in_=AP(bncS, bcoff, [[0, 128], [1, 1]]))
                P.dsig(inst, "dG")
            return P.c["dG"]

        pa_mm = []
        pa_ev = []
        spillA = []
        for wsel, (wago, spd) in enumerate([(agUo, upb), (agGo, gab)]):
            for oc in range(2):
                @block.sync
                def _(y, wago=wago, oc=oc, wsel=wsel):
                    if pa_mm:
                        y.wait_ge(P.h["tT"], pa_mm[-1])
                    if wsel == 0 and oc == 0:
                        y.wait_ge(P.h["dG"], wag_done)
                        if not sim_nocc:
                            y.wait_ge(P.h["cc"], wag_cc)
                        for jj in range(6):
                            inst = y.dma_start(
                                out=AP(Ppad, 0, [[1024, 512], [1, 1024]]),
                                in_=wqb1.ap()[0:512, :])
                            P.dsig(inst, "dY")
                        y.wait_ge(P.h["dY"], P.c["dY"])
                    for dt in range(NPT):
                        inst = y.dma_start(
                            out=fwt[:, dt * 1536:(dt + 1) * 1536],
                            in_=wago.ap()[oc * 1536:(oc + 1) * 1536,
                                          dt * 128:(dt + 1) * 128],
                            transpose=True)
                        P.dsig(inst, "dY")
                    inst = y.dma_start(out=AP(Ppad, 0, [[128, 128], [1, 128]]),
                                       in_=fwt[:, 8 * 1536 - 128:8 * 1536])
                    P.dsig(inst, "dY")
                fwt_ld = P.c["dY"]
                for tc in range(4):
                    for ot in range(12):
                        idx = (wsel * 2 + oc) * 48 + tc * 12 + ot
                        bk = idx % 2
                        slot = idx % 4

                        @block.tensor
                        def _(t, oc=oc, tc=tc, ot=ot, bk=bk, fwt_ld=fwt_ld,
                              idx=idx):
                            t.wait_ge(P.h["dY"], fwt_ld)
                            t.wait_ge(P.h["tV"], xq1_done)
                            if idx >= 2:
                                t.wait_ge(P.h["tV"], pa_ev[idx - 2])
                            for dt in range(NPT):
                                inst = t.matmul(
                                    pF1[bk][0:128, 0:512],
                                    fwt[:, dt * 1536 + ot * 128:
                                        dt * 1536 + ot * 128 + 128],
                                    xnv(dt, tc * 512, (tc + 1) * 512),
                                    start=(dt == 0), stop=(dt == NPT - 1))
                            pa_mm.append(P.sig(inst, "tT"))

                        @block.vector
                        def _(v, oc=oc, tc=tc, ot=ot, bk=bk, idx=idx,
                              wsel=wsel, slot=slot):
                            P.wait(v, "tT")
                            if idx >= 4:
                                v.wait_ge(P.h["dG"], spillA[idx - 4])
                            v.tensor_copy(fo[:, slot * 512:(slot + 1) * 512],
                                          pF1[bk][0:128, 0:512])
                            ridx = oc * 48 + tc * 12 + ot
                            inst = v.tensor_reduce(
                                fred[:, ridx:ridx + 1], pF1[bk][0:128, 0:512],
                                AX.X, ALU.max, apply_absolute_value=True)
                            pa_ev.append(P.sig(inst, "tV"))

                        @block.gpsimd
                        def _(g, oc=oc, tc=tc, ot=ot, idx=idx, spd=spd,
                              slot=slot):
                            g.wait_ge(P.h["tV"], pa_ev[idx])
                            inst = g.dma_start(
                                out=spd.ap()[oc * 1536 + ot * 128:
                                             oc * 1536 + (ot + 1) * 128,
                                             tc * 512:(tc + 1) * 512],
                                in_=fo[:, slot * 512:(slot + 1) * 512])
                            spillA.append(P.dsig(inst, "dG"))
            @block.vector
            def _(v, wsel=wsel):
                v.memset(fred[:, 96:98], 0.0)
                v.memset(fred[:, 98:100], 0.0)
                inst = v.tensor_reduce(fcols[:, wsel * 4:wsel * 4 + 1],
                                       fred[:, 0:96], AX.X, ALU.max)
                P.sig(inst, "tV")

        up_cc = l1_amax_cc(0, 1, 250)
        up_sc = scale_cols(1, 2, 16, 32767.0, up_cc)
        ga_cc = l1_amax_cc(4, 5, 251)
        ga_sc = scale_cols(5, 6, 18, 32767.0, ga_cc)

        pb_ld = []
        pb_sig = []
        pb_sp = []
        pb_sgm = []
        for ot in range(24):
            for tc in range(4):
                idx = ot * 4 + tc
                sl = idx % 2

                @block.gpsimd
                def _(g, ot=ot, tc=tc, sl=sl, idx=idx, ga_sc=ga_sc):
                    if idx == 0:
                        g.wait_ge(P.h["dG"], P.c["dG"])
                        for jj in range(4):
                            inst = g.dma_start(
                                out=AP(Ppad, 0, [[1536, 512], [1, 1536]]),
                                in_=xnf32.ap()[0:512, 0:1536])
                            P.dsig(inst, "dG")
                        g.wait_ge(P.h["dG"], P.c["dG"])
                    if idx >= 2:
                        g.wait_ge(P.h["tV"], pb_sig[idx - 2])
                    inst = g.dma_start(
                        out=fa[:, sl * 512:(sl + 1) * 512],
                        in_=gab.ap()[ot * 128:(ot + 1) * 128,
                                     tc * 512:(tc + 1) * 512])
                    P.dsig(inst, "dG")
                    inst = g.dma_start(out=AP(bncE, 2048, [[16, 128], [1, 16]]),
                                       in_=fa[:, sl * 512 + 496:(sl + 1) * 512])
                    pb_ld.append(P.dsig(inst, "dG"))

                @block.vector
                def _(v, sl=sl, idx=idx, ga_sc=ga_sc):
                    v.wait_ge(P.h["dG"], pb_ld[idx])
                    if idx == 0:
                        v.wait_ge(P.h["tV"], ga_sc)
                    v.tensor_scalar(fb[:, sl * 512:(sl + 1) * 512],
                                    fa[:, sl * 512:(sl + 1) * 512],
                                    fcols[:, 18:19], None, ALU.mult)
                    v.tensor_copy(fi[:, sl * 512:(sl + 1) * 512],
                                  fb[:, sl * 512:(sl + 1) * 512])
                    inst = v.tensor_scalar(fb[:, sl * 512:(sl + 1) * 512],
                                           fi[:, sl * 512:(sl + 1) * 512],
                                           fcols[:, 6:7], None, ALU.mult)
                    pb_sgm.append(P.sig(inst, "tV"))

                @block.scalar
                def _(s, sl=sl, idx=idx):
                    s.wait_ge(P.h["tV"], pb_sgm[idx])
                    inst = s.activation(fa[:, sl * 512:(sl + 1) * 512],
                                        fb[:, sl * 512:(sl + 1) * 512],
                                        AF.Sigmoid)
                    P.sig(inst, "tS")
                sgm = P.c["tS"]

                @block.vector
                def _(v, sl=sl, idx=idx, sgm=sgm, ot=ot, tc=tc):
                    v.wait_ge(P.h["tS"], sgm)
                    v.tensor_scalar(fa[:, sl * 512:(sl + 1) * 512],
                                    fa[:, sl * 512:(sl + 1) * 512],
                                    65536.0, None, ALU.mult)
                    v.tensor_copy(fu[:, sl * 512:(sl + 1) * 512],
                                  fa[:, sl * 512:(sl + 1) * 512])
                    v.tensor_scalar(fa[:, sl * 512:(sl + 1) * 512],
                                    fu[:, sl * 512:(sl + 1) * 512],
                                    1.0 / 65536.0, None, ALU.mult)
                    v.tensor_tensor(fa[:, sl * 512:(sl + 1) * 512],
                                    fa[:, sl * 512:(sl + 1) * 512],
                                    fb[:, sl * 512:(sl + 1) * 512], ALU.mult)
                    ridx = idx % 96
                    v.tensor_reduce(fred[:, ridx:ridx + 1],
                                    fa[:, sl * 512:(sl + 1) * 512], AX.X,
                                    ALU.max, apply_absolute_value=True)
                    if idx >= 4:
                        v.wait_ge(P.h["dG"], pb_sp[idx - 4])
                    inst = v.tensor_copy(fo[:, (idx % 4) * 512:
                                         (idx % 4 + 1) * 512],
                                         fa[:, sl * 512:(sl + 1) * 512])
                    pb_sig.append(P.sig(inst, "tV"))

                @block.gpsimd
                def _(g, idx=idx, ot=ot, tc=tc):
                    g.wait_ge(P.h["tV"], pb_sig[idx])
                    inst = g.dma_start(
                        out=tb1.ap()[ot * 128:(ot + 1) * 128,
                                     tc * 512:(tc + 1) * 512],
                        in_=fo[:, (idx % 4) * 512:(idx % 4 + 1) * 512])
                    pb_sp.append(P.dsig(inst, "dG"))

        @block.vector
        def _(v):
            v.wait_ge(P.h["dG"], pb_sp[-1])
            v.memset(fred[:, 96:98], 0.0)
            v.memset(fred[:, 98:100], 0.0)
            inst = v.tensor_reduce(fcols[:, 8:9], fred[:, 0:96], AX.X, ALU.max)
            P.sig(inst, "tV")
        tcc = l1_amax_cc(8, 9, 252)
        t_sc = scale_cols(9, 10, 20, 32767.0, tcc)

        pc_ld = []
        pc_sig = []
        pc_sp = []
        for ot in range(24):
            for tc in range(4):
                idx = ot * 4 + tc
                sl = idx % 2

                @block.gpsimd
                def _(g, ot=ot, tc=tc, sl=sl, idx=idx,
                      pbs=(pb_sp[-1] if pb_sp else None)):
                    if idx == 0:
                        g.wait_ge(P.h["dG"], pbs)
                        for jj in range(4):
                            inst = g.dma_start(
                                out=AP(Ppad, 0, [[1536, 512], [1, 1536]]),
                                in_=xnf32.ap()[0:512, 0:1536])
                            P.dsig(inst, "dG")
                        g.wait_ge(P.h["dG"], P.c["dG"])
                    if idx >= 2:
                        g.wait_ge(P.h["tV"], pc_sig[idx - 2])
                    inst = g.dma_start(
                        out=fa[:, sl * 512:(sl + 1) * 512],
                        in_=tb1.ap()[ot * 128:(ot + 1) * 128,
                                     tc * 512:(tc + 1) * 512])
                    P.dsig(inst, "dG")
                    inst = g.dma_start(
                        out=fb[:, (2 + sl) * 512:(3 + sl) * 512],
                        in_=upb.ap()[ot * 128:(ot + 1) * 128,
                                     tc * 512:(tc + 1) * 512])
                    P.dsig(inst, "dG")
                    inst = g.dma_start(out=AP(bncE, 2048, [[16, 128], [1, 16]]),
                                       in_=fb[:, (2 + sl) * 512 + 496:
                                              (3 + sl) * 512])
                    pc_ld.append(P.dsig(inst, "dG"))

                @block.vector
                def _(v, sl=sl, idx=idx, t_sc=t_sc, up_sc=up_sc, ot=ot, tc=tc):
                    v.wait_ge(P.h["dG"], pc_ld[idx])
                    if idx == 0:
                        v.wait_ge(P.h["tV"], t_sc)
                        v.wait_ge(P.h["tV"], up_sc)
                    v.tensor_scalar(fb[:, sl * 512:(sl + 1) * 512],
                                    fa[:, sl * 512:(sl + 1) * 512],
                                    fcols[:, 20:21], None, ALU.mult)
                    v.tensor_copy(fi[:, sl * 512:(sl + 1) * 512],
                                  fb[:, sl * 512:(sl + 1) * 512])
                    v.tensor_scalar(fa[:, sl * 512:(sl + 1) * 512],
                                    fi[:, sl * 512:(sl + 1) * 512],
                                    fcols[:, 10:11], None, ALU.mult)
                    v.tensor_scalar(fb[:, sl * 512:(sl + 1) * 512],
                                    fb[:, (2 + sl) * 512:(3 + sl) * 512],
                                    fcols[:, 16:17], None, ALU.mult)
                    v.tensor_copy(fi[:, (2 + sl) * 512:(3 + sl) * 512],
                                  fb[:, sl * 512:(sl + 1) * 512])
                    v.tensor_scalar(fb[:, sl * 512:(sl + 1) * 512],
                                    fi[:, (2 + sl) * 512:(3 + sl) * 512],
                                    fcols[:, 2:3], None, ALU.mult)
                    v.tensor_tensor(fa[:, sl * 512:(sl + 1) * 512],
                                    fa[:, sl * 512:(sl + 1) * 512],
                                    fb[:, sl * 512:(sl + 1) * 512], ALU.mult)
                    ridx = idx % 96
                    v.tensor_reduce(fred[:, ridx:ridx + 1],
                                    fa[:, sl * 512:(sl + 1) * 512], AX.X,
                                    ALU.max, apply_absolute_value=True)
                    if idx >= 4:
                        v.wait_ge(P.h["dG"], pc_sp[idx - 4])
                    inst = v.tensor_copy(fo[:, (idx % 4) * 512:
                                         (idx % 4 + 1) * 512],
                                         fa[:, sl * 512:(sl + 1) * 512])
                    pc_sig.append(P.sig(inst, "tV"))

                @block.gpsimd
                def _(g, idx=idx, ot=ot, tc=tc):
                    g.wait_ge(P.h["tV"], pc_sig[idx])
                    inst = g.dma_start(
                        out=ob1.ap()[ot * 128:(ot + 1) * 128,
                                     tc * 512:(tc + 1) * 512],
                        in_=fo[:, (idx % 4) * 512:(idx % 4 + 1) * 512])
                    pc_sp.append(P.dsig(inst, "dG"))

        @block.vector
        def _(v):
            v.wait_ge(P.h["dG"], pc_sp[-1])
            v.memset(fred[:, 96:98], 0.0)
            v.memset(fred[:, 98:100], 0.0)
            inst = v.tensor_reduce(fcols[:, 12:13], fred[:, 0:96], AX.X,
                                   ALU.max)
            P.sig(inst, "tV")
        occ = l1_amax_cc(12, 13, 253)
        o_sc = scale_cols(13, 14, 22, 32767.0, occ)

        pd_ld = []
        pd_q = []
        pd_mm = []
        pd_ev = [None] * 8
        pDall = pF1 + pD
        for tc in range(4):
            @block.gpsimd
            def _(g, tc=tc, o_sc=o_sc, pcs=(pc_sp[-1] if pc_sp else None)):
                if tc == 0:
                    g.wait_ge(P.h["dG"], pcs)
                    for jj in range(4):
                        inst = g.dma_start(
                            out=AP(Ppad, 0, [[1536, 512], [1, 1536]]),
                            in_=xnf32.ap()[0:512, 0:1536])
                        P.dsig(inst, "dG")
                    g.wait_ge(P.h["dG"], P.c["dG"])
                if tc >= 1:
                    g.wait_ge(P.h["tT"], pd_mm[-1])
                for dt in range(24):
                    gix = tc * 24 + dt
                    if gix >= 2:
                        g.wait_ge(P.h["tV"], pd_q[gix - 2])
                    inst = g.dma_start(
                        out=fa[:, (dt % 2) * 512:(dt % 2 + 1) * 512],
                        in_=ob1.ap()[dt * 128:(dt + 1) * 128,
                                     tc * 512:(tc + 1) * 512])
                    P.dsig(inst, "dG")
                    inst = g.dma_start(out=AP(bncE, 2048, [[16, 128], [1, 16]]),
                                       in_=fa[:, (dt % 2) * 512 + 496:
                                              (dt % 2 + 1) * 512])
                    pd_ld.append(P.dsig(inst, "dG"))

                    @block.vector
                    def _(v, tc=tc, dt=dt, o_sc=o_sc):
                        v.wait_ge(P.h["dG"], pd_ld[tc * 24 + dt])
                        if tc == 0 and dt == 0:
                            v.wait_ge(P.h["tV"], o_sc)
                        sl = dt % 2
                        v.tensor_scalar(fb[:, sl * 512:(sl + 1) * 512],
                                        fa[:, sl * 512:(sl + 1) * 512],
                                        fcols[:, 22:23], None, ALU.mult)
                        v.tensor_copy(fi[:, sl * 512:(sl + 1) * 512],
                                      fb[:, sl * 512:(sl + 1) * 512])
                        inst = v.tensor_scalar(
                            fo[:, dt * 512:(dt + 1) * 512],
                            fi[:, sl * 512:(sl + 1) * 512],
                            fcols[:, 14:15], None, ALU.mult)
                        pd_q.append(P.sig(inst, "tV"))

            for half in range(2):
                @block.sync
                def _(y, tc=tc, half=half):
                    if pd_mm:
                        y.wait_ge(P.h["tT"], pd_mm[-1])
                    else:
                        y.wait_ge(P.h["tT"], pa_mm[-1])
                    for d2 in range(12):
                        dt2 = half * 12 + d2
                        inst = y.dma_start(
                            out=fwt[:, d2 * 1024:(d2 + 1) * 1024],
                            in_=agDo.ap()[0:1024, dt2 * 128:(dt2 + 1) * 128],
                            transpose=True)
                        P.dsig(inst, "dY")
                    inst = y.dma_start(out=AP(Ppad, 0, [[128, 128], [1, 128]]),
                                       in_=fwt[:, 12 * 1024 - 128:12 * 1024])
                    P.dsig(inst, "dY")
                fwd_ld = P.c["dY"]

                for ot in range(NPT):
                    @block.tensor
                    def _(t, tc=tc, half=half, ot=ot, fwd_ld=fwd_ld):
                        t.wait_ge(P.h["dY"], fwd_ld)
                        t.wait_ge(P.h["tV"], pd_q[tc * 24 + 23])
                        if half == 0 and pd_ev[ot] is not None:
                            t.wait_ge(P.h["tV"], pd_ev[ot])
                        for d2 in range(12):
                            dt2 = half * 12 + d2
                            inst = t.matmul(
                                pDall[ot][0:128, 0:512],
                                fwt[:, d2 * 1024 + ot * 128:
                                    d2 * 1024 + ot * 128 + 128],
                                fo[:, dt2 * 512:(dt2 + 1) * 512],
                                start=(half == 0 and d2 == 0),
                                stop=(half == 1 and d2 == 11))
                        pd_mm.append(P.sig(inst, "tT"))

                    if half == 1:
                        @block.vector
                        def _(v, tc=tc, ot=ot):
                            P.wait(v, "tT")
                            inst = v.tensor_tensor(
                                xr(ot, tc * 512, (tc + 1) * 512),
                                xr(ot, tc * 512, (tc + 1) * 512),
                                pDall[ot][0:128, 0:512], ALU.add)
                            pd_ev[ot] = P.sig(inst, "tV")
        ffn1_done = P.c["tV"]

        if dbg_hw:
            @block.gpsimd
            def _(g):
                g.wait_ge(P.h["tV"], ffn1_done)
                inst = g.dma_start(out=dbgo.ap()[:, 0:512], in_=xn[:, 0:512])
                P.dsig(inst, "dG")
                inst = g.dma_start(out=dbgo.ap()[0:65, 512:1024],
                                   in_=kTg[0][0:65, 0:512])
                P.dsig(inst, "dG")
                inst = g.dma_start(out=dbgo.ap()[65:128, 512:1024],
                                   in_=kTg[0][0:63, 2048:2560])
                P.dsig(inst, "dG")
                inst = g.dma_start(out=dbgo.ap()[:, 1040:1064],
                                   in_=fcols[:, 0:24])
                P.dsig(inst, "dG")
                inst = g.dma_start(out=dbgo.ap()[:, 1024:1040], in_=cols[:, 0:16])
                P.dsig(inst, "dG")
                inst = g.dma_start(out=dbgo.ap()[0:1, 1040:1552],
                                   in_=bncS.ap()[0:1, 0:512])
                P.dsig(inst, "dG")
                inst = g.dma_start(out=dbgo.ap()[1:2, 1040:1056], in_=cc_in.ap())
                P.dsig(inst, "dG")
                inst = g.dma_start(out=dbgo.ap()[2:3, 1040:1056], in_=cc_out.ap())
                P.dsig(inst, "dG")
                g.wait_ge(P.h["dG"], P.c["dG"])

        @block.sync
        def _(y):
            y.wait_ge(P.h["tV"], ffn1_done)
            for p in range(NPT):
                inst = y.dma_start(out=outT.ap()[128 * p:128 * (p + 1), :],
                                   in_=xr(p, 0, TPC))
                P.dsig(inst, "dO")
            y.wait_ge(P.h["dO"], P.c["dO"])
        f1p.close()
        f1.close()
        return nc


def kernel(**inputs):
    x = np.asarray(inputs["L0_attn_x"], np.float32)
    w_qkv = np.asarray(inputs["L0_attn_w_qkv"], np.float32)
    w_out = np.asarray(inputs["L0_attn_w_out"], np.float32)
    W0 = np.asarray(inputs["L0_ffn_weight"], np.float32)
    b0 = np.ascontiguousarray(
        np.asarray(inputs["L0_ffn_bias"], np.float32).reshape(8, 128).T)
    Wq1 = np.asarray(inputs["L1_attn_Wq"], np.float32)
    Wk1 = np.asarray(inputs["L1_attn_Wk"], np.float32)
    Wv1 = np.asarray(inputs["L1_attn_Wv"], np.float32)
    Wo1 = np.asarray(inputs["L1_attn_Wo"], np.float32)
    EdT = np.ascontiguousarray(np.asarray(inputs["L1_attn_E"], np.float32).T)
    Wu1 = np.asarray(inputs["L1_ffn_w_up"], np.float32)
    Wg1 = np.asarray(inputs["L1_ffn_w_gate"], np.float32)
    Wd1 = np.asarray(inputs["L1_ffn_w_down"], np.float32)
    in_maps = []
    for c in range(NCORE):
        b_, hf = c // 2, c % 2
        own = x[b_, hf * TPC:(hf + 1) * TPC]
        haloo = (np.zeros((HALO, D), np.float32) if hf == 0 else
                 np.ascontiguousarray(x[b_, TPC - HALO:TPC]))
        ext = np.concatenate([haloo, own], 0)
        m0v = np.full((128, 4), -1e30 if hf == 0 else 0.0, np.float32)
        pr1v = np.zeros((128, 2), np.float32)
        pr1v[:, 0] = 1.0 if hf == 0 else 0.0
        pr1v[:, 1] = 1.0 - pr1v[:, 0]
        in_maps.append({
            "xT": np.ascontiguousarray(ext.T),
            "w_qkv": w_qkv, "w_out": w_out, "W0": W0, "b0": b0, "m0": m0v,
            "Wq1": Wq1, "Wk1": Wk1, "Wv1": Wv1, "Wo1": Wo1, "EdT": EdT,
            "pr1": pr1v,
            "Wu1": Wu1, "Wg1": Wg1, "Wd1": Wd1,
        })
    nc = build_program(dbg_hw=globals().get("_DBG_HW", False))
    res = run_bass_kernel_spmd(nc, in_maps, core_ids=list(range(NCORE)))
    if globals().get("_DBG_HW", False):
        np.save("/root/problem/_dbgo.npy",
                np.stack([res.results[c]["dbgo"] for c in range(NCORE)]))
    out = np.empty((B, S, D), np.float32)
    for c in range(NCORE):
        b_, hf = c // 2, c % 2
        out[b_, hf * TPC:(hf + 1) * TPC] = res.results[c]["outT"].T
    return out
